# version 21
# baseline (speedup 1.0000x reference)
import numpy as np
import ml_dtypes
import concourse.bass as bass
import concourse.mybir as mybir
from concourse.bass_utils import run_bass_kernel_spmd

F32 = mybir.dt.float32
I32 = mybir.dt.int32
BF16 = mybir.dt.bfloat16
AF = mybir.ActivationFunctionType
OP = mybir.AluOpType
AX = mybir.AxisListType

H = 1024
TILE = 512
NJ = 4
EPS = 1e-5
ENGINES = ("sync", "tensor", "vector", "scalar", "gpsimd")
SIGMA = [2, 3, 0, 1]
SIGSGN = [+1, -1, -1, +1]
PERM = [4, 5, 6, 7, 0, 1, 2, 3]
PSGN = [+1, +1, -1, -1, -1, -1, +1, +1]


class Buf:
    __slots__ = ("w", "rs", "frozen", "pend_r", "pend_w", "pend_q")

    def __init__(self):
        self.w = None
        self.rs = []
        self.frozen = False
        self.pend_r = False
        self.pend_w = False
        self.pend_q = None

    def rd(self, tok):
        if not self.frozen and tok is not None:
            self.rs.append(tok)

    def rw(self):
        assert not self.pend_w, "read of DMA-write-pending buffer before fence"
        return [self.w] if self.w is not None else []

    def ww(self):
        assert not (self.pend_w or self.pend_r), \
            "write of DMA-pending buffer before fence"
        ws = list(self.rs)
        if self.w is not None:
            ws.append(self.w)
        return ws

    def wrote(self, tok):
        if not self.frozen:
            self.w = tok
            self.rs = []

    def freeze(self):
        self.frozen = True
        self.rs = []


class KB:
    def __init__(self, n_tiles=8, debug=False):
        self.NT = n_tiles
        self.L = TILE * n_tiles
        self.debug = debug
        nc = bass.Bass("TRN2", target_bir_lowering=False)
        self.nc = nc
        self.q = {e: [] for e in ENGINES}
        self.h = {}
        self.cnt = {}
        self._gd_pending = []
        self._q_pending = []
        self._ctx = []
        for n in ("T", "V", "S", "G", "Q", "GD", "QF"):
            g = nc.semaphore(n)
            self.h[n] = g.__enter__()
            self._ctx.append(g)
            self.cnt[n] = 0
        self._declare()
        self.phase = "init"
        self.te_labels = []

    def _sched(self, eng, fn, waits, sem, incn=1):
        self.cnt[sem] += incn
        tok = (sem, self.cnt[sem], eng)
        wd = {}
        for w in waits:
            if w is None:
                continue
            s, v, weng = w
            if eng == "tensor" and weng == "tensor":
                continue
            assert v <= self.cnt[s], f"future wait {s}:{v}"
            if wd.get(s, 0) < v:
                wd[s] = v
        self.q[eng].append((tuple(wd.items()), fn, (sem, incn)))
        return tok

    def step(self, eng, fn, reads=(), writes=(), extra=()):
        waits = list(extra)
        for b in reads:
            waits += b.rw()
        for b in writes:
            waits += b.ww()
        sem = {"sync": "Q", "tensor": "T", "vector": "V",
               "scalar": "S", "gpsimd": "G"}[eng]
        tok = self._sched(eng, fn, waits, sem)
        for b in reads:
            b.rd(tok)
        for b in writes:
            b.wrote(tok)
        return tok

    def dma(self, qname, out, in_, reads=(), writes=()):
        waits = []
        for b in reads:
            if b.pend_q == qname:
                continue
            waits += b.rw()
        for b in writes:
            if b.pend_q == qname:
                continue
            waits += b.ww()
        eng, sem = (("sync", "Q") if qname == "q" else ("gpsimd", "GD"))
        waits = [w for w in waits if w is None or w[0] != sem]
        tok = self._sched(eng, lambda e, o=out, i=in_: e.dma_start(out=o, in_=i),
                          waits, sem, 16)
        pend = self._gd_pending if qname == "g" else self._q_pending
        for b in reads:
            assert b.pend_q in (None, qname), "buffer pending on other queue"
            pend.append((b, "r"))
            b.pend_r = True
            b.pend_q = qname
        for b in writes:
            assert b.pend_q in (None, qname), "buffer pending on other queue"
            pend.append((b, "w"))
            b.pend_w = True
            b.pend_q = qname
        return tok

    def _fence(self, pend, eng, dmasem, fsem):
        if not pend:
            return None
        ftok = self._sched(eng, lambda e: e.nop(),
                           [(dmasem, self.cnt[dmasem], "dma")], fsem)
        for b, kind in pend:
            b.pend_r = False
            b.pend_w = False
            b.pend_q = None
        for b, kind in pend:
            if kind == "w":
                b.wrote(ftok)
        for b, kind in pend:
            if kind == "r":
                b.rd(ftok)
        del pend[:]
        return ftok

    def g_fence(self):
        return self._fence(self._gd_pending, "gpsimd", "GD", "G")

    def q_fence(self):
        return self._fence(self._q_pending, "sync", "Q", "QF")

    def mm(self, bank, psum_ap, lhsT, rhs, start, stop, reads=()):
        self.te_labels.append(self.phase)
        waits = []
        for b in reads:
            waits += b.rw()
        waits += bank.ww() if start else bank.rw()
        tok = self._sched("tensor",
                          lambda e, ps=psum_ap, l=lhsT, r=rhs, st=start, sp=stop:
                          e.matmul(ps, l, r, start=st, stop=sp), waits, "T")
        for b in reads:
            b.rd(tok)
        bank.wrote(tok)
        return tok

    def v(self, fn, reads=(), writes=(), extra=()):
        return self.step("vector", fn, reads, writes, extra)

    def s(self, fn, reads=(), writes=(), extra=()):
        return self.step("scalar", fn, reads, writes, extra)

    def g(self, fn, reads=(), writes=(), extra=()):
        return self.step("gpsimd", fn, reads, writes, extra)

    def emit(self, blk):
        for e in ENGINES:
            if not self.q[e]:
                continue
            queue = self.q[e]

            def body(eng, queue=queue):
                seen = {}
                for waits, fn, inc in queue:
                    for (s, v) in waits:
                        if seen.get(s, 0) >= v:
                            continue
                        eng.wait_ge(self.h[s], v)
                        seen[s] = v
                    ins = fn(eng)
                    if inc is not None:
                        ins.then_inc(self.h[inc[0]], inc[1])
            getattr(blk, e)(body)
        self.q = {e: [] for e in ENGINES}

    def _declare(self):
        nc, L = self.nc, self.L
        D = {}
        def din(name, shape, dt=F32):
            D[name] = nc.dram_tensor(name, shape, dt, kind="ExternalInput").ap()
        din("x0", [L, H])
        din("cn_w1", [H, H]); din("cn_b1", [H])
        din("cn_w2", [H, 512]); din("cn_b2", [512])
        din("cn_w3", [512, 256]); din("cn_b3", [256])
        din("cn_w4", [256, 1]); din("cn_b4", [1])
        din("gc_w1", [H, 512]); din("gc_b1", [512])
        din("gc_w2", [512, 256]); din("gc_b2", [256])
        din("gc_w3", [256, 1]); din("gc_b3", [1])
        din("ar", [1])
        din("steps_raw", [256]); din("G_raw", [256]); din("A_raw", [256])
        din("B2", [256, 2 * H]); din("C2", [H, 512]); din("D_vec", [H])
        din("ltri", [128, 128]); din("identf", [128, 128])
        din("identb", [128, 128], BF16)
        din("onescol_f", [128, 1]); din("onescol_b", [128, 1], BF16)
        din("onesrow_b", [1, 128], BF16); din("onesrow_f", [1, 128])
        din("consts", [1, 4])
        D["y"] = nc.dram_tensor("y", [L, H], F32, kind="ExternalOutput").ap()
        D["x2scr"] = nc.dram_tensor("x2scr", [L, H], F32, kind="Internal").ap()
        if self.debug:
            for nm in ("dbg_x1", "dbg_x2", "dbg_u1"):
                D[nm] = nc.dram_tensor(nm, [L, H], F32, kind="ExternalOutput").ap()
        self.D = D

        A = {}
        def sb(name, shape, dt):
            g = nc.sbuf_tensor(name, shape, dt)
            A[name] = g.__enter__()
            self._ctx.append(g)
        sb("xw0", [128, 4096], F32); sb("xw1", [128, 4096], F32)
        sb("u1TM", [128, 4096], BF16); sb("u2t", [128, 3072], BF16)
        sb("u2FMt", [128, 1024], BF16)
        sb("uFM", [128, 4096], BF16); sb("uFMn", [128, 2048], BF16)
        sb("mA", [128, 4096], BF16); sb("mB", [128, 4096], BF16)
        sb("h1", [128, 4096], BF16); sb("h2", [128, 2048], BF16)
        sb("h3", [128, 1024], BF16); sb("h1p", [128, 1024], BF16)
        sb("stile", [128, 1024], BF16)
        sb("w1s", [128, 8192], BF16)
        sb("w2s", [128, 4096], BF16); sb("w3s", [128, 1024], BF16)
        sb("w4s", [128, 2], BF16)
        sb("gc3", [128, 2], BF16)
        sb("Bh", [128, 4096], BF16); sb("Ch", [128, 4096], BF16)
        sb("Dbc", [128, 1024], BF16)
        sb("b1c", [128, 8], F32); sb("b2c", [128, 4], F32)
        sb("b3c", [128, 2], F32); sb("b4c", [1, 1], F32)
        sb("gb1c", [128, 4], F32); sb("gb2c", [128, 2], F32)
        sb("gb3c", [1, 1], F32); sb("arw", [1, 1], F32)
        sb("R1", [128, 512], F32); sb("R2", [128, 512], F32)
        sb("Q21s", [128, 512], F32); sb("Q22s", [128, 512], F32)
        sb("G1s", [128, 512], F32); sb("G2s", [128, 512], F32)
        sb("h2s", [128, 512], BF16)
        sb("ysT", [128, 2048], BF16)
        sb("ltri_s", [128, 128], F32); sb("idf", [128, 128], F32)
        sb("idb", [128, 128], BF16)
        sb("oncf", [128, 1], F32); sb("oncb", [128, 1], BF16)
        sb("on1b", [1, 128], BF16); sb("on1f", [1, 128], F32)
        sb("cst", [1, 4], F32)
        sb("rw", [1, 1024], F32)
        sb("erow", [1, 2048], F32)
        sb("trow", [1, 1024], F32)
        sb("srow", [1, 1024], F32)
        sb("crow", [1, 2048], F32)
        sb("abrow", [1, 1024], F32)
        sb("sumrow", [1, 1024], F32)
        sb("brow", [1, 2048], BF16)
        sb("statS", [128, 4], F32); sb("statQ", [128, 4], F32)
        sb("a1c", [128, 32], F32); sb("bb1c", [128, 32], F32)
        sb("a2c", [128, 32], F32); sb("bb2c", [128, 32], F32)
        sb("a3c", [128, 32], F32); sb("bb3c", [128, 32], F32)
        sb("stt1", [128, 8], F32); sb("stt2", [128, 8], F32)
        sb("stt3", [128, 8], F32)
        sb("meanu", [128, 8], BF16)
        sb("gcol1", [128, 4], BF16); sb("gcol2", [128, 2], BF16)
        sb("gfv", [1, 1], F32); sb("kap", [1, 2], F32); sb("hb", [1, 2], F32)
        sb("bbsum", [1, 1], F32)
        sb("stt4", [128, 8], F32)
        sb("abc", [128, 12], F32)
        sb("cbcA", [128, 512], BF16); sb("cbcB", [128, 512], BF16)
        sb("PT11", [128, 256], F32); sb("PT12", [128, 256], F32)
        sb("PT21", [128, 256], F32); sb("PT22", [128, 256], F32)
        sb("RP1", [128, 256], F32); sb("RP2", [128, 256], F32)
        sb("lad", [128, 16], F32); sb("ladQ", [128, 16], F32)
        sb("mcols", [128, 24], F32)
        self.A = A
        self.PS = []
        for i in range(8):
            g = nc.psum_tensor(f"P{i}", [128, 512], F32)
            self.PS.append(g.__enter__())
            self._ctx.append(g)
        names = list(A) + [f"P{i}" for i in range(8)] + ["xwA", "xwB"]
        for drop in ("xw0", "xw1"):
            names.remove(drop)
        self.B = {k: Buf() for k in names}
        SPLIT = dict(mA=8, mB=8, h1=8, h2=4, h3=2, uFM=8, stile=2, rw=2)
        COARSE = set("uFM".split())
        for nm, n in SPLIT.items():
            if nm in COARSE:
                b = Buf()
                self.B[nm] = [b] * n
            else:
                self.B[nm] = [Buf() for _ in range(n)]
        for nm in ("xwA", "xwB"):
            if nm in COARSE:
                b = Buf()
                self.B[nm] = [b] * NJ
            else:
                self.B[nm] = [Buf() for _ in range(NJ)]
        self.B["srow2"] = Buf()
        self.B["u2t"] = [Buf() for _ in range(3)]
        self.B["u1TM"] = [Buf() for _ in range(NJ)]
        for nm in ("stt1", "stt2", "stt3", "stt4", "statS", "statQ"):
            self.B[nm] = [Buf() for _ in range(NJ)]

    def build(self):
        self.setup()
        self.sweep1()
        self.sweep2()
        self.sweep3()
        self.q["sync"].append(
            (((("Q", self.cnt["Q"]), ("GD", self.cnt["GD"]))),
             lambda e: e.nop(), None))
        with self.nc.Block() as blk:
            self.emit(blk)
        return self.nc

    def setup(self):
        self.phase = "setup"
        A, B, D, PS = self.A, self.B, self.D, self.PS
        dma, v, s, mm = self.dma, self.v, self.s, self.mm
        xw0, xw1 = A["xw0"], A["xw1"]
        XA, XB = B["xwA"], B["xwB"]

        for nm, dst in [("consts", "cst"), ("ltri", "ltri_s"), ("identf", "idf"),
                        ("onescol_f", "oncf"), ("onesrow_f", "on1f"),
                        ("ar", "arw"), ("cn_b4", "b4c"), ("gc_b3", "gb3c"),
                        ("identb", "idb"), ("onescol_b", "oncb"),
                        ("onesrow_b", "on1b")]:
            dma("q", A[dst][:], D[nm], writes=[B[dst]])
        for c in range(8):
            dma("q", A["b1c"][:, c:c + 1], D["cn_b1"][128 * c:128 * (c + 1)],
                writes=[B["b1c"]])
        for c in range(4):
            dma("q", A["b2c"][:, c:c + 1], D["cn_b2"][128 * c:128 * (c + 1)],
                writes=[B["b2c"]])
            dma("q", A["gb1c"][:, c:c + 1], D["gc_b1"][128 * c:128 * (c + 1)],
                writes=[B["gb1c"]])
        for c in range(2):
            dma("q", A["b3c"][:, c:c + 1], D["cn_b3"][128 * c:128 * (c + 1)],
                writes=[B["b3c"]])
            dma("q", A["gb2c"][:, c:c + 1], D["gc_b2"][128 * c:128 * (c + 1)],
                writes=[B["gb2c"]])
        SL = {}
        for i, nm in enumerate(["sri", "gri", "ari", "ST", "GG", "AA", "SS",
                                "SI", "AL", "RA", "RB", "RC", "RD", "STI",
                                "F1C", "F2C"]):
            SL[nm] = xw0[0:1, 256 * i:256 * (i + 1)]
        for i, nm in enumerate(["MA", "MB", "MC", "MD"]):
            SL[nm] = xw1[0:1, 256 * i:256 * (i + 1)]
        SL["Drow"] = A["srow"][0:1, 0:1024]
        self._SL = SL
        dma("q", SL["sri"], D["steps_raw"], writes=[*XA])
        dma("q", SL["gri"], D["G_raw"], writes=[*XA])
        dma("q", SL["ari"], D["A_raw"], writes=[*XA])
        dma("q", SL["Drow"], D["D_vec"], writes=[B["srow"]])
        self.q_fence()
        v(lambda e: e.tensor_scalar(A["hb"][0:1, 0:1], A["b4c"][:], 0.5, None, OP.mult),
          reads=[B["b4c"]], writes=[B["hb"]])
        v(lambda e: e.tensor_scalar(A["hb"][0:1, 1:2], A["gb3c"][:], 0.5, None, OP.mult),
          reads=[B["gb3c"]], writes=[B["hb"]])
        s(lambda e: e.activation(SL["ST"], SL["sri"], AF.Tanh, scale=0.5),
          reads=[*XA], writes=[*XA])
        v(lambda e: e.tensor_scalar(SL["ST"], SL["ST"], 0.5, 0.5, OP.mult, OP.add),
          reads=[*XA], writes=[*XA])
        v(lambda e: e.tensor_scalar(SL["GG"], SL["gri"], 0.0, None, OP.max),
          reads=[*XA], writes=[*XA])
        v(lambda e: e.tensor_scalar(SL["AA"], SL["ari"], 0.0, None, OP.max),
          reads=[*XA], writes=[*XA])
        v(lambda e: e.tensor_tensor(SL["SS"], SL["ST"], SL["GG"], OP.mult),
          reads=[*XA], writes=[*XA])
        v(lambda e: e.tensor_scalar(SL["SS"], SL["SS"], 1.0, None, OP.add),
          reads=[*XA], writes=[*XA])
        v(lambda e: e.reciprocal(SL["SI"], SL["SS"]), reads=[*XA], writes=[*XA])
        v(lambda e: e.tensor_tensor(SL["AL"], SL["ST"], SL["ST"], OP.mult),
          reads=[*XA], writes=[*XA])
        v(lambda e: e.tensor_tensor(SL["AL"], SL["AL"], SL["AA"], OP.mult),
          reads=[*XA], writes=[*XA])
        v(lambda e: e.tensor_tensor(SL["AL"], SL["AL"], SL["SI"], OP.mult),
          reads=[*XA], writes=[*XA])
        s(lambda e: e.activation(SL["AL"], SL["AL"], AF.Tanh, scale=1.0 / 1.99),
          reads=[*XA], writes=[*XA])
        v(lambda e: e.tensor_scalar(SL["AL"], SL["AL"], 1.99, None, OP.mult),
          reads=[*XA], writes=[*XA])
        v(lambda e: e.tensor_copy(SL["RA"], SL["SI"]), reads=[*XA], writes=[*XA])
        v(lambda e: e.reciprocal(SL["STI"], SL["ST"]), reads=[*XA], writes=[*XA])
        v(lambda e: e.tensor_tensor(SL["RB"], SL["AL"], SL["STI"], OP.mult),
          reads=[*XA], writes=[*XA])
        v(lambda e: e.tensor_scalar(SL["RB"], SL["RB"], -1.0, None, OP.mult),
          reads=[*XA], writes=[*XA])
        v(lambda e: e.tensor_tensor(SL["RC"], SL["ST"], SL["SI"], OP.mult),
          reads=[*XA], writes=[*XA])
        v(lambda e: e.tensor_scalar(SL["RD"], SL["AL"], -1.0, 1.0, OP.mult, OP.add),
          reads=[*XA], writes=[*XA])
        v(lambda e: e.tensor_copy(SL["F1C"], SL["RC"]), reads=[*XA], writes=[*XA])
        v(lambda e: e.tensor_tensor(SL["F2C"], SL["ST"], SL["RC"], OP.mult),
          reads=[*XA], writes=[*XA])
        v(lambda e: e.tensor_tensor(SL["MA"], SL["RD"], SL["SS"], OP.mult),
          reads=[*XA], writes=[*XB])
        v(lambda e: e.tensor_tensor(SL["MB"], SL["RB"], SL["SS"], OP.mult),
          reads=[*XA], writes=[*XB])
        v(lambda e: e.tensor_scalar(SL["MB"], SL["MB"], -1.0, None, OP.mult),
          reads=[*XB], writes=[*XB])
        v(lambda e: e.tensor_tensor(SL["MC"], SL["RC"], SL["SS"], OP.mult),
          reads=[*XA], writes=[*XB])
        v(lambda e: e.tensor_scalar(SL["MC"], SL["MC"], -1.0, None, OP.mult),
          reads=[*XB], writes=[*XB])
        v(lambda e: e.tensor_tensor(SL["MD"], SL["RA"], SL["SS"], OP.mult),
          reads=[*XA], writes=[*XB])

        order = ["MA", "MB", "MC", "MD", "RA", "RB", "RC", "RD", "F1C", "F2C"]
        bank = B["P0"]
        for idx, nm in enumerate(order):
            sl = SL[nm]
            for ch in range(2):
                self.mm(bank, PS[0][:, 2 * idx + ch:2 * idx + ch + 1],
                        sl[0:1, 128 * ch:128 * (ch + 1)],
                        A["cst"][0:1, 2:3], True, True,
                        reads=[*XA, *XB, B["cst"]])
        v(lambda e: e.tensor_copy(A["mcols"][:, 0:20], PS[0][:, 0:20]),
          reads=[bank], writes=[B["mcols"]])
        MCI = {"ai": 0, "bi": 2, "ci": 4, "di": 6, "a": 8, "b": 10,
               "c": 12, "d": 14, "f1c": 16, "f2c": 18}
        def mc(nm, ch):
            return A["mcols"][:, MCI[nm] + ch:MCI[nm] + ch + 1]

        self._mc = mc


        for k in range(8):
            dma("g", A["w1s"][:, 1024 * k:1024 * (k + 1)],
                D["cn_w1"][128 * k:128 * (k + 1), :], writes=[B["w1s"]])
            dma("g", A["w2s"][:, 512 * k:512 * (k + 1)],
                D["cn_w2"][128 * k:128 * (k + 1), :], writes=[B["w2s"]])
        for k in range(4):
            dma("g", A["w3s"][:, 256 * k:256 * (k + 1)],
                D["cn_w3"][128 * k:128 * (k + 1), :], writes=[B["w3s"]])
        for k in range(2):
            dma("g", A["w4s"][:, k:k + 1], D["cn_w4"][128 * k:128 * (k + 1), :],
                writes=[B["w4s"]])
            dma("g", A["gc3"][:, k:k + 1], D["gc_w3"][128 * k:128 * (k + 1), :],
                writes=[B["gc3"]])
        for pc in range(2):
            dma("g", A["mA"][:, 2048 * pc:2048 * (pc + 1)],
                D["B2"][128 * pc:128 * (pc + 1), :], writes=[*B["mA"]])
        for hk in range(8):
            dma("g", A["mB"][:, 512 * hk:512 * (hk + 1)],
                D["C2"][128 * hk:128 * (hk + 1), :], writes=[*B["mB"]])
        self.g_fence()

        v(lambda e: e.memset(A["rw"][:], 0.0), writes=[*B["rw"]])
        v(lambda e: e.memset(A["sumrow"][:], 0.0), writes=[B["sumrow"]])

        for k in ("w1s", "w2s", "w3s", "w4s", "gc3",
                  "Bh", "Ch", "Dbc", "b1c", "b2c", "b3c", "b4c", "gb1c",
                  "gb2c", "gb3c", "arw", "R1", "R2", "Q21s", "Q22s",
                  "ltri_s", "idf", "idb", "oncf", "oncb", "on1b", "on1f",
                  "cst", "erow", "mcols"):
            self.B[k].freeze()

    def setup2_steps(self):
        self.phase = "setup2"
        """Scan-tile construction etc. -- emitted after sweep1 so its long
        serial VE chains overlap sweep-2's head."""
        A, B, D, PS = self.A, self.B, self.D, self.PS
        dma, v, s, mm = self.dma, self.v, self.s, self.mm
        xw0, xw1 = A["xw0"], A["xw1"]
        XA, XB = B["xwA"], B["xwB"]
        SL = self._SL
        mc = self._mc
        for pc in range(2):
            for hk in range(8):
                yield
                for t in range(2):
                    st_col = 2048 * pc + 256 * hk + t
                    bank = B["P3"]
                    self.mm(bank, PS[3][:, 0:128],
                            A["mA"][:, st_col:st_col + 255:2], A["idb"][:],
                            True, True, reads=[*B["mA"], B["idb"]])
                    v(lambda e, hk=hk, t=t, pc=pc:
                      e.tensor_copy(A["Bh"][:, 512 * hk + 256 * t + 128 * pc:
                                            512 * hk + 256 * t + 128 * (pc + 1)],
                                    PS[3][:, 0:128]),
                      reads=[bank], writes=[B["Bh"]])
        for hk in range(8):
            yield
            for qq in range(4):
                t, pc = qq // 2, qq % 2
                st_col = 512 * hk + 256 * pc + t
                bank = B["P4"]
                self.mm(bank, PS[4][:, 0:128],
                        A["mB"][:, st_col:st_col + 255:2], A["idb"][:],
                        True, True, reads=[*B["mB"], B["idb"]])
                v(lambda e, hk=hk, qq=qq, t=t:
                  e.tensor_scalar(A["Ch"][:, 1024 * qq + 128 * hk:1024 * qq + 128 * (hk + 1)],
                                  PS[4][:, 0:128], 1.0 if t == 0 else -1.0,
                                  None, OP.mult),
                  reads=[bank], writes=[B["Ch"]])

        v(lambda e: e.tensor_copy(A["brow"][0:1, 1536:2048], SL["Drow"][0:1, 0:512]),
          reads=[B["srow"]], writes=[B["brow"]])
        v(lambda e: e.tensor_copy(A["brow"][0:1, 1024:1536], SL["Drow"][0:1, 512:1024]),
          reads=[B["srow"]], writes=[B["brow"]])
        for hh in range(2):
            bank = B["P5"]
            self.mm(bank, PS[5][:, 0:512], A["on1b"][:],
                    A["brow"][0:1, (1536 if hh == 0 else 1024):
                              (2048 if hh == 0 else 1536)],
                    True, True, reads=[B["on1b"], B["brow"]])
            v(lambda e, hh=hh:
              e.tensor_copy(A["Dbc"][:, 512 * hh:512 * (hh + 1)], PS[5][:, 0:512]),
              reads=[bank], writes=[B["Dbc"]])


        def build_powers(tiles, ladname, e11, e12, e21, e22, bottom_only):
            lt = A[ladname]
            LB = B[ladname]
            for i, nm in enumerate((e11, e12, e21, e22)):
                for ch in range(2):
                    v(lambda e, i=i, ch=ch, nm=nm:
                      e.tensor_copy(lt[:, 2 * i + ch:2 * i + ch + 1], mc(nm, ch)),
                      reads=[B["mcols"]], writes=[LB])
            use = ["11", "12", "21", "22"] if not bottom_only else ["21", "22"]
            for nm in use:
                T = tiles[nm]
                for ch in range(2):
                    val = 1.0 if nm in ("11", "22") else 0.0
                    v(lambda e, T=T, ch=ch, val=val:
                      e.memset(A[T][:, 128 * ch:128 * ch + 1], val),
                      writes=[B[T]])
            yield
            d, par = 1, 0
            while d < 128:
                for ch in range(2):
                    o = 128 * ch
                    for nm, (sa, sb_, ia, ib) in {
                        "11": ("11", "12", 0, 2), "12": ("11", "12", 1, 3),
                        "21": ("21", "22", 0, 2), "22": ("21", "22", 1, 3),
                    }.items():
                        if bottom_only and nm not in ("21", "22"):
                            continue
                        TO, TA_, TB_ = tiles[nm], tiles[sa], tiles[sb_]
                        la = lt[:, 8 * par + 2 * ia:8 * par + 2 * ia + 1] \
                            if ch == 0 else lt[:, 8 * par + 2 * ia + 1:8 * par + 2 * ia + 2]
                        lb = lt[:, 8 * par + 2 * ib:8 * par + 2 * ib + 1] \
                            if ch == 0 else lt[:, 8 * par + 2 * ib + 1:8 * par + 2 * ib + 2]
                        v(lambda e, TO=TO, TA_=TA_, la=la, o=o, d=d:
                          e.tensor_scalar(A[TO][:, o + d:o + 2 * d],
                                          A[TA_][:, o:o + d], la, None, OP.mult),
                          reads=[B[TA_], LB], writes=[B[TO]])
                        v(lambda e, TO=TO, TB_=TB_, lb=lb, o=o, d=d:
                          e.scalar_tensor_tensor(A[TO][:, o + d:o + 2 * d],
                                                 A[TB_][:, o:o + d], lb,
                                                 A[TO][:, o + d:o + 2 * d],
                                                 OP.mult, OP.add),
                          reads=[B[TB_], LB], writes=[B[TO]])
                np_ = 1 - par
                for ch in range(2):
                    li = lambda i, ch=ch, par=par: lt[:, 8 * par + 2 * i + ch:8 * par + 2 * i + ch + 1]
                    lo = lambda i, ch=ch, np_=np_: lt[:, 8 * np_ + 2 * i + ch:8 * np_ + 2 * i + ch + 1]
                    for oi, (x1_, y1, x2_, y2) in enumerate(
                            [(0, 0, 1, 2), (0, 1, 1, 3), (2, 0, 3, 2), (2, 1, 3, 3)]):
                        v(lambda e, lo=lo, li=li, oi=oi, x1_=x1_, y1=y1:
                          e.tensor_tensor(lo(oi), li(x1_), li(y1), OP.mult),
                          reads=[LB], writes=[LB])
                        v(lambda e, lo=lo, li=li, oi=oi, x2_=x2_, y2=y2:
                          e.scalar_tensor_tensor(lo(oi), li(x2_), li(y2), lo(oi),
                                                 OP.mult, OP.add),
                          reads=[LB], writes=[LB])
                par = np_
                d *= 2
                yield
            return par

        parP = yield from build_powers({"11": "PT11", "12": "PT12", "21": "PT21",
                             "22": "PT22"}, "lad", "ai", "bi", "ci", "di", False)

        for ch in range(2):
            o = 128 * ch
            v(lambda e, o=o, ch=ch:
              e.tensor_scalar(A["RP1"][:, o:o + 128], A["PT11"][:, o:o + 128],
                              mc("f1c", ch), None, OP.mult),
              reads=[B["PT11"], B["mcols"]], writes=[B["RP1"]])
            v(lambda e, o=o, ch=ch:
              e.scalar_tensor_tensor(A["RP1"][:, o:o + 128], A["PT12"][:, o:o + 128],
                                     mc("f2c", ch), A["RP1"][:, o:o + 128],
                                     OP.mult, OP.add),
              reads=[B["PT12"], B["mcols"]], writes=[B["RP1"]])
            v(lambda e, o=o, ch=ch:
              e.tensor_scalar(A["RP2"][:, o:o + 128], A["PT21"][:, o:o + 128],
                              mc("f1c", ch), None, OP.mult),
              reads=[B["PT21"], B["mcols"]], writes=[B["RP2"]])
            v(lambda e, o=o, ch=ch:
              e.scalar_tensor_tensor(A["RP2"][:, o:o + 128], A["PT22"][:, o:o + 128],
                                     mc("f2c", ch), A["RP2"][:, o:o + 128],
                                     OP.mult, OP.add),
              reads=[B["PT22"], B["mcols"]], writes=[B["RP2"]])
        yield
        parQ = yield from build_powers({"11": "PT11", "12": "PT12", "21": "PT21",
                             "22": "PT22"}, "ladQ", "a", "b", "c", "d", True)
        for src, dst in [("RP1", "R1"), ("RP2", "R2"),
                         ("PT21", "Q21s"), ("PT22", "Q22s")]:
            yield
            for ch in range(2):
                bank = B["P1"]
                self.mm(bank, PS[1][:, 0:128], A[src][:, 128 * ch:128 * (ch + 1)],
                        A["idf"][:], True, True, reads=[B[src], B["idf"]])
                for t in range(2):
                    v(lambda e, dst=dst, t=t, ch=ch:
                      e.tensor_copy(A[dst][:, 256 * t + 128 * ch:256 * t + 128 * (ch + 1)],
                                    PS[1][:, 0:128]),
                      reads=[bank], writes=[B[dst]])

        for i in range(4):
            yield
            for ch in range(2):
                bank = B["P2"]
                self.mm(bank, PS[2][0:1, 0:128],
                        A["ladQ"][:, 8 * parQ + 2 * i + ch:8 * parQ + 2 * i + ch + 1],
                        A["idf"][:], True, True, reads=[B["ladQ"], B["idf"]])
                for t in range(2):
                    v(lambda e, i=i, ch=ch, t=t:
                      e.tensor_copy(A["erow"][0:1, 512 * i + 256 * t + 128 * ch:
                                              512 * i + 256 * t + 128 * (ch + 1)],
                                    PS[2][0:1, 0:128]),
                      reads=[bank], writes=[B["erow"]])

    def xw(self, t):
        return (self.A["xw0"], self.B["xwA"]) if t % 2 == 0 else \
               (self.A["xw1"], self.B["xwB"])

    def load_xtile(self, t, src="x0"):
        xwA, xwB = self.xw(t)
        for j in range(NJ):
            self.dma("q", xwA[:, 1024 * j:1024 * (j + 1)],
                     self.D[src][TILE * t + 128 * j:TILE * t + 128 * (j + 1), :],
                     writes=[xwB[j]])
            self.q_fence()

    def u1slot(self, t, j):
        k = (4 * t + j) % self.u1mod
        if k < 4:
            return self.A["u1TM"][:, 1024 * k:1024 * (k + 1)], self.B["u1TM"][k]
        if k == 4:
            return self.A["h1p"][:], self.B["h1p"]
        return (self.A["u2t"][:, 1024 * (k - 5):1024 * (k - 4)],
                self.B["u2t"][k - 5])

    def ln_stats_j(self, t, j):
        A, B = self.A, self.B
        xwA, xwB = self.xw(t)
        self.s(lambda e, j=j: e.activation(A["u2FMt"][:], xwA[:, 1024 * j:1024 * (j + 1)],
                                           AF.Square, accum_out=A["statQ"][:, j:j + 1]),
               reads=[xwB[j]], writes=[B["u2FMt"], B["statQ"][j]])
        self.v(lambda e, j=j: e.reduce_sum(A["statS"][:, j:j + 1],
                                           xwA[:, 1024 * j:1024 * (j + 1)], axis=AX.X),
               reads=[xwB[j]], writes=[B["statS"][j]])

    def ln_statmath_j(self, t, j, ac, bbc):
        A, B = self.A, self.B
        col = (4 * t + j) % 32
        s1 = A["stt1"][:, j:j + 1]
        s2 = A["stt2"][:, j:j + 1]
        s3 = A["stt3"][:, j:j + 1]
        acs = A[ac][:, col:col + 1]
        bbs = A[bbc][:, col:col + 1]
        v, s = self.v, self.s
        v(lambda e: e.tensor_scalar(s1, A["statS"][:, j:j + 1], 1.0 / H, None, OP.mult),
          reads=[B["statS"][j]], writes=[B["stt1"][j]])
        v(lambda e: e.tensor_scalar(s2, A["statQ"][:, j:j + 1], 1.0 / H, EPS,
                                    OP.mult, OP.add),
          reads=[B["statQ"][j]], writes=[B["stt2"][j]])
        v(lambda e: e.tensor_tensor(s3, s1, s1, OP.mult),
          reads=[B["stt1"][j]], writes=[B["stt3"][j]])
        v(lambda e: e.tensor_sub(s2, s2, s3),
          reads=[B["stt3"][j]], writes=[B["stt2"][j]])
        v(lambda e: e.reciprocal(s3, s2),
          reads=[B["stt2"][j]], writes=[B["stt3"][j]])
        s(lambda e: e.activation(acs, s3, AF.Sqrt),
          reads=[B["stt3"][j]], writes=[B[ac]])
        v(lambda e: e.scalar_tensor_tensor(bbs, s1, -1.0, acs, OP.mult, OP.mult),
          reads=[B["stt1"][j], B[ac]], writes=[B[bbc]])

    def u_build(self, t, j, uT, uTB, ac, bbc, dst_j=None, eng="g"):
        A, B = self.A, self.B
        xwA, xwB = self.xw(t)
        col = (4 * t + j) % 32
        if uT is None:
            dap, dbf = self.u1slot(t, j)
        else:
            dj = j if dst_j is None else dst_j
            dap, dbf = uT[:, 1024 * dj:1024 * (dj + 1)], uTB[dj]
        f = {"g": self.g, "v": self.v}[eng]
        f(lambda e: e.tensor_scalar(dap, xwA[:, 1024 * j:1024 * (j + 1)],
                                    A[ac][:, col:col + 1],
                                    A[bbc][:, col:col + 1],
                                    OP.mult, OP.add),
          reads=[xwB[j], B[ac], B[bbc]], writes=[dbf])

    def transpose_uFM_j(self, t, j, uT, uTB, negate=False):
        self.phase = f"tpuFM.t{t}.j{j}"
        """u1 (TM) -> uFM chunk-columns for this j (FFN banks P5/P6)."""
        A, B, PS = self.A, self.B, self.PS
        if uT is None:
            sap, sbf = self.u1slot(t, j)
        else:
            sap, sbf = uT[:, 1024 * j:1024 * (j + 1)], uTB[j]
        b0 = getattr(self, "tpu_banks", (5, 6))
        for half in range(2):
            pb = b0[half]
            bank = B[f"P{pb}"]
            for cc in range(4):
                c = 4 * half + cc
                self.mm(bank, PS[pb][:, 128 * cc:128 * (cc + 1)],
                        sap[:, 128 * c:128 * (c + 1)],
                        A["idb"][:], cc == 0, cc == 3, reads=[sbf, B["idb"]])
            for cc in range(4):
                c = 4 * half + cc
                fn = (lambda e, c=c, j=j, pb=pb, cc=cc:
                      e.tensor_copy(A["uFM"][:, 512 * c + 128 * j:512 * c + 128 * (j + 1)],
                                    PS[pb][:, 128 * cc:128 * (cc + 1)]))
                fns = (lambda e, c=c, j=j, pb=pb, cc=cc:
                       e.copy(A["uFM"][:, 512 * c + 128 * j:512 * c + 128 * (j + 1)],
                              PS[pb][:, 128 * cc:128 * (cc + 1)]))
                if cc % 2 == 0:
                    self.v(fn, reads=[bank], writes=[B["uFM"][c]])
                else:
                    self.s(fns, reads=[bank], writes=[B["uFM"][c]])
        if negate:
            self.v(lambda e, j=j: e.tensor_scalar(
                A["uFMn"][:, 128 * j:128 * (j + 1)],
                A["uFM"][:, 3072 + 128 * j:3072 + 128 * (j + 1)], -1.0, None, OP.mult),
                reads=[B["uFM"][6]], writes=[B["uFMn"]])
            self.v(lambda e, j=j: e.tensor_scalar(
                A["uFMn"][:, 512 + 128 * j:512 + 128 * (j + 1)],
                A["uFM"][:, 3584 + 128 * j:3584 + 128 * (j + 1)], -1.0, None, OP.mult),
                reads=[B["uFM"][7]], writes=[B["uFMn"]])
            self.v(lambda e, j=j: e.tensor_scalar(
                A["uFMn"][:, 1024 + 128 * j:1024 + 128 * (j + 1)],
                A["uFM"][:, 128 * j:128 * (j + 1)], -1.0, None, OP.mult),
                reads=[B["uFM"][0]], writes=[B["uFMn"]])
            self.v(lambda e, j=j: e.tensor_scalar(
                A["uFMn"][:, 1536 + 128 * j:1536 + 128 * (j + 1)],
                A["uFM"][:, 512 + 128 * j:512 + 128 * (j + 1)], -1.0, None, OP.mult),
                reads=[B["uFM"][1]], writes=[B["uFMn"]])

    def u2_prep(self, t, j):
        self.phase = f"u2prep.t{t}.j{j}"
        """u2 for scan chunk j of tile t: build u2t (TM) + transpose to
        u2FMt (feature-major) via SSM bank P0."""
        A, B, PS = self.A, self.B, self.PS
        aj = j % 3
        self.u_build(t, j, A["u2t"], B["u2t"], "a2c", "bb2c", dst_j=aj)
        u2B = B["u2t"][aj]
        for grp in range(2):
            bank = B["P0"]
            for cc in range(4):
                c = 4 * grp + cc
                self.mm(bank, PS[0][:, 128 * cc:128 * (cc + 1)],
                        A["u2t"][:, 1024 * aj + 128 * c:1024 * aj + 128 * (c + 1)],
                        A["idb"][:], cc == 0, cc == 3, reads=[u2B, B["idb"]])
            for cc in range(4):
                c = 4 * grp + cc
                if cc % 2 == 0:
                    self.v(lambda e, c=c, cc=cc:
                           e.tensor_copy(A["u2FMt"][:, 128 * c:128 * (c + 1)],
                                         PS[0][:, 128 * cc:128 * (cc + 1)]),
                           reads=[bank], writes=[B["u2FMt"]])
                else:
                    self.s(lambda e, c=c, cc=cc:
                           e.copy(A["u2FMt"][:, 128 * c:128 * (c + 1)],
                                  PS[0][:, 128 * cc:128 * (cc + 1)]),
                           reads=[bank], writes=[B["u2FMt"]])

    def ffn_main(self, t, uT, uTB, feeder=None):
        fl = f"ffn.{getattr(self, 'sweep_tag', '?')}.t{t}"
        def F():
            if feeder:
                ph = self.phase
                feeder()
                self.phase = ph
        """mA/mB + three coupling cycles -> abc coeff columns.
        Banks: P5/P6 for matmul groups, P7 for rows/bcast/coeffs.
        slot(i) is called at 4 TE-idle points to interleave other work."""
        A, B, PS = self.A, self.B, self.PS
        v, s, g, mm = self.v, self.s, self.g, self.mm
        cr = A["crow"]
        ab = A["abrow"]

        def mm1_chain(which):
            self.phase = f"{fl}.mm1{which}"
            for m in range(8):
                pi = 5 + m % 2
                bank = B[f"P{pi}"]
                for k in range(8):
                    lhsT = A["w1s"][:, 1024 * k + 128 * m:1024 * k + 128 * (m + 1)]
                    if which == "A":
                        mov, mb = A["uFM"][:, 512 * k:512 * (k + 1)], B["uFM"][k]
                    elif PSGN[k] > 0:
                        mov, mb = A["uFM"][:, 512 * PERM[k]:512 * (PERM[k] + 1)], \
                            B["uFM"][PERM[k]]
                    elif k in (2, 3):
                        mov, mb = A["uFMn"][:, 512 * (k - 2):512 * (k - 1)], B["uFMn"]
                    else:
                        mov, mb = A["uFMn"][:, 1024 + 512 * (k - 4):1024 + 512 * (k - 3)], \
                            B["uFMn"]
                    mm(bank, PS[pi][:, 0:512], lhsT, mov, k == 0, k == 7,
                       reads=[B["w1s"], mb])
                dst = "mA" if which == "A" else "mB"
                s(lambda e, m=m, dst=dst, pi=pi:
                  e.copy(A[dst][:, 512 * m:512 * (m + 1)], PS[pi][:, 0:512]),
                  reads=[bank], writes=[B[dst][m]])
                if m % 2 == 1:
                    F()
                if which == "A":
                    s(lambda e, m=m, pi=pi:
                      e.activation(A["h1"][:, 512 * m:512 * (m + 1)],
                                   PS[pi][:, 0:512], AF.Gelu,
                                   bias=A["b1c"][:, m:m + 1]),
                      reads=[bank, B["b1c"]], writes=[B["h1"][m]])

        def mm234(it):
            self.phase = f"{fl}.mm234.{it}"
            for m in range(4):
                bank = B[f"P{5 + m % 2}"]
                for k in range(8):
                    mm(bank, PS[5 + m % 2][:, 0:512],
                       A["w2s"][:, 512 * k + 128 * m:512 * k + 128 * (m + 1)],
                       A["h1"][:, 512 * k:512 * (k + 1)], k == 0, k == 7,
                       reads=[B["w2s"], B["h1"][k]])
                s(lambda e, m=m: e.activation(A["h2"][:, 512 * m:512 * (m + 1)],
                                              PS[5 + m % 2][:, 0:512], AF.Gelu,
                                              bias=A["b2c"][:, m:m + 1]),
                  reads=[bank, B["b2c"]], writes=[B["h2"][m]])
                F()
            for m in range(2):
                bank = B[f"P{5 + m % 2}"]
                for k in range(4):
                    mm(bank, PS[5 + m % 2][:, 0:512],
                       A["w3s"][:, 256 * k + 128 * m:256 * k + 128 * (m + 1)],
                       A["h2"][:, 512 * k:512 * (k + 1)], k == 0, k == 3,
                       reads=[B["w3s"], B["h2"][k]])
                s(lambda e, m=m: e.activation(A["h3"][:, 512 * m:512 * (m + 1)],
                                              PS[5 + m % 2][:, 0:512], AF.Gelu,
                                              bias=A["b3c"][:, m:m + 1]),
                  reads=[bank, B["b3c"]], writes=[B["h3"][m]])
                F()
            bank = B["P7"]
            for k in range(2):
                mm(bank, PS[7][0:1, 0:512], A["w4s"][:, k:k + 1],
                   A["h3"][:, 512 * k:512 * (k + 1)], k == 0, k == 1,
                   reads=[B["w4s"], B["h3"][k]])
            s(lambda e: e.activation(cr[0:1, 1536:2048], PS[7][0:1, 0:512],
                                     AF.Tanh, scale=0.5, bias=A["hb"][0:1, 0:1]),
              reads=[bank, B["hb"]], writes=[B["crow"]])
            v(lambda e, it=it:
              e.tensor_scalar(cr[0:1, 512 * it:512 * (it + 1)], cr[0:1, 1536:2048],
                              A["kap"][0:1, 1:2], A["kap"][0:1, 0:1],
                              OP.mult, OP.add),
              reads=[B["kap"]], writes=[B["crow"]])
            F()

        def combines(it):
            bank = B["P7"]
            if it == 1:
                v(lambda e: e.tensor_copy(A["brow"][0:1, 0:512], cr[0:1, 0:512]),
                  reads=[B["crow"]], writes=[B["brow"]])
                mm(bank, PS[7][:, 0:512], A["on1b"][:], A["brow"][0:1, 0:512],
                   True, True, reads=[B["on1b"], B["brow"]])
                v(lambda e: e.tensor_copy(A["cbcB"][:], PS[7][:, 0:512]),
                  reads=[bank], writes=[B["cbcB"]])
            else:
                v(lambda e: e.tensor_tensor(ab[0:1, 0:512], cr[0:1, 512:1024],
                                            cr[0:1, 0:512], OP.mult),
                  reads=[B["crow"]], writes=[B["abrow"]])
                v(lambda e: e.tensor_scalar(ab[0:1, 0:512], ab[0:1, 0:512],
                                            -1.0, 1.0, OP.mult, OP.add),
                  reads=[], writes=[B["abrow"]])
                v(lambda e: e.tensor_add(ab[0:1, 512:1024], cr[0:1, 0:512],
                                         cr[0:1, 512:1024]),
                  reads=[B["crow"]], writes=[B["abrow"]])
                v(lambda e: e.tensor_copy(A["brow"][0:1, 512:1024], ab[0:1, 0:512]),
                  reads=[B["abrow"]], writes=[B["brow"]])
                v(lambda e: e.tensor_copy(A["brow"][0:1, 1024:1536], ab[0:1, 512:1024]),
                  reads=[B["abrow"]], writes=[B["brow"]])
                mm(bank, PS[7][:, 0:512], A["on1b"][:], A["brow"][0:1, 512:1024],
                   True, True, reads=[B["on1b"], B["brow"]])
                v(lambda e: e.tensor_copy(A["cbcA"][:], PS[7][:, 0:512]),
                  reads=[bank], writes=[B["cbcA"]])
                mm(bank, PS[7][:, 0:512], A["on1b"][:], A["brow"][0:1, 1024:1536],
                   True, True, reads=[B["on1b"], B["brow"]])
                v(lambda e: e.tensor_copy(A["cbcB"][:], PS[7][:, 0:512]),
                  reads=[bank], writes=[B["cbcB"]])
            cbc2B = A["cbcB"][:].rearrange("p f -> p () f").broadcast_to([128, 2, 512])
            cbc2A = A["cbcA"][:].rearrange("p f -> p () f").broadcast_to([128, 2, 512])
            for pr in range(4):
                m0 = 2 * pr
                sl = lambda T, pr=pr: A[T][:, 1024 * pr:1024 * (pr + 1)].rearrange(
                    "p (r f) -> p r f", r=2)
                hv = A["h1"][:, 1024 * pr:1024 * (pr + 1)].rearrange(
                    "p (r f) -> p r f", r=2)
                sc = A["h2"][:, 1024 * (pr % 2):1024 * (pr % 2 + 1)].rearrange(
                    "p (r f) -> p r f", r=2)
                scb = [B["h2"][2 * (pr % 2)], B["h2"][2 * (pr % 2) + 1]]
                hb = [B["h1"][m0], B["h1"][m0 + 1]]
                eng = v if pr != 3 else g
                if it == 1:
                    eng(lambda e, sl=sl, hv=hv, cbc2B=cbc2B:
                        e.tensor_tensor(hv, sl("mB"), cbc2B, OP.mult),
                        reads=[B["mB"][m0], B["mB"][m0 + 1], B["cbcB"]],
                        writes=hb)
                    eng(lambda e, sl=sl, hv=hv:
                        e.tensor_tensor(hv, hv, sl("mA"), OP.add),
                        reads=[B["mA"][m0], B["mA"][m0 + 1]], writes=hb)
                else:
                    eng(lambda e, sl=sl, hv=hv, cbc2A=cbc2A:
                        e.tensor_tensor(hv, sl("mA"), cbc2A, OP.mult),
                        reads=[B["mA"][m0], B["mA"][m0 + 1], B["cbcA"]],
                        writes=hb)
                    eng(lambda e, sl=sl, sc=sc, cbc2B=cbc2B:
                        e.tensor_tensor(sc, sl("mB"), cbc2B, OP.mult),
                        reads=[B["mB"][m0], B["mB"][m0 + 1], B["cbcB"]],
                        writes=scb)
                    eng(lambda e, hv=hv, sc=sc:
                        e.tensor_tensor(hv, hv, sc, OP.add),
                        reads=scb, writes=hb)
                for mm_ in (m0, m0 + 1):
                    s(lambda e, m=mm_: e.activation(
                        A["h1"][:, 512 * m:512 * (m + 1)],
                        A["h1"][:, 512 * m:512 * (m + 1)], AF.Gelu,
                        bias=A["b1c"][:, m:m + 1]),
                      reads=[B["h1"][mm_], B["b1c"]], writes=[B["h1"][mm_]])

        mm1_chain("A")
        mm234(0)
        mm1_chain("B")
        combines(1)
        F()
        mm234(1)
        combines(2)
        F()
        mm234(2)

        v(lambda e: e.tensor_tensor(A["srow"][0:1, 0:512], cr[0:1, 1024:1536],
                                    ab[0:1, 512:1024], OP.mult),
          reads=[B["crow"], B["abrow"]], writes=[B["srow"]])
        v(lambda e: e.tensor_tensor(A["srow"][0:1, 512:1024], cr[0:1, 1024:1536],
                                    ab[0:1, 0:512], OP.mult),
          reads=[B["crow"], B["abrow"]], writes=[B["srow"]])
        v(lambda e: e.tensor_sub(cr[0:1, 0:512], ab[0:1, 0:512], A["srow"][0:1, 0:512]),
          reads=[B["abrow"], B["srow"]], writes=[B["crow"]])
        v(lambda e: e.tensor_add(cr[0:1, 512:1024], ab[0:1, 512:1024],
                                 A["srow"][0:1, 512:1024]),
          reads=[B["abrow"], B["srow"]], writes=[B["crow"]])
        bank = B["P7"]
        for jj in range(NJ):
            mm(bank, PS[7][:, jj:jj + 1], cr[0:1, 128 * jj:128 * (jj + 1)],
               A["cst"][0:1, 0:1], True, True, reads=[B["crow"], B["cst"]])
            mm(bank, PS[7][:, 4 + jj:5 + jj], cr[0:1, 512 + 128 * jj:512 + 128 * (jj + 1)],
               A["cst"][0:1, 0:1], True, True, reads=[B["crow"], B["cst"]])
            mm(bank, PS[7][:, 8 + jj:9 + jj], cr[0:1, 512 + 128 * jj:512 + 128 * (jj + 1)],
               A["cst"][0:1, 1:2], True, True, reads=[B["crow"], B["cst"]])
        v(lambda e: e.tensor_copy(A["abc"][:], PS[7][:, 0:12]),
          reads=[bank], writes=[B["abc"]])

    def ffn_apply_j(self, t, j, uT, uTB):
        A, B = self.A, self.B
        xwA, xwB = self.xw(t)
        v = self.v
        if uT is None:
            sap, sbf = self.u1slot(t, j)
        else:
            sap, sbf = uT[:, 1024 * j:1024 * (j + 1)], uTB[j]
        v(lambda e, j=j:
          e.tensor_scalar(A["stile"][:], sap,
                          A["abc"][:, j:j + 1], None, OP.mult),
          reads=[sbf, B["abc"]], writes=[*B["stile"]])
        for qq in range(4):
            colb = (4 + j) if SIGSGN[qq] > 0 else (8 + j)
            v(lambda e, j=j, qq=qq, colb=colb:
              e.scalar_tensor_tensor(A["stile"][:, 256 * qq:256 * (qq + 1)],
                                     sap[:, 256 * SIGMA[qq]:256 * (SIGMA[qq] + 1)],
                                     A["abc"][:, colb:colb + 1],
                                     A["stile"][:, 256 * qq:256 * (qq + 1)],
                                     OP.mult, OP.add),
              reads=[sbf, B["abc"]], writes=[B["stile"][qq // 2]])
        v(lambda e, j=j:
          e.tensor_add(xwA[:, 1024 * j:1024 * (j + 1)],
                       xwA[:, 1024 * j:1024 * (j + 1)], A["stile"][:]),
          reads=[*B["stile"]], writes=[xwB[j]])

    def gf_chain(self, bbc="bb1c"):
        self.phase = "gf"
        A, B, PS, D = self.A, self.B, self.PS, self.D
        v, s, mm = self.v, self.s, self.mm
        for k in range(8):
            self.dma("g", A["mA"][:, 512 * k:512 * (k + 1)],
                     D["gc_w1"][128 * k:128 * (k + 1), :], writes=[B["mA"][k]])
        for k in range(4):
            self.dma("g", A["mB"][:, 256 * k:256 * (k + 1)],
                     D["gc_w2"][128 * k:128 * (k + 1), :],
                     writes=[B["mB"][k // 2]])
        self.g_fence()
        bank = B["P5"]
        if bbc is not None:
            nbc = min(4 * self.NT, 32)
            v(lambda e: e.reduce_sum(A["stt4"][:, 4:5], A[bbc][:, 0:nbc], axis=AX.X),
              reads=[B[bbc]], writes=[B["stt4"][0]])
            mm(bank, PS[5][0:1, 8:9], A["stt4"][:, 4:5], A["oncf"][:], True, True,
               reads=[B["stt4"][0], B["oncf"]])
            v(lambda e: e.tensor_copy(A["bbsum"][:], PS[5][0:1, 8:9]),
              reads=[bank], writes=[B["bbsum"]])
            v(lambda e: e.tensor_scalar(A["sumrow"][:], A["sumrow"][:],
                                        A["bbsum"][0:1, 0:1], None, OP.add),
              reads=[B["bbsum"]], writes=[B["sumrow"]])
        for kk in range(8):
            mm(bank, PS[5][:, kk:kk + 1],
               A["sumrow"][0:1, 128 * kk:128 * (kk + 1)],
               A["cst"][0:1, 3:4], True, True, reads=[B["sumrow"], B["cst"]])
        v(lambda e: e.tensor_copy(A["meanu"][:], PS[5][:, 0:8]),
          reads=[bank], writes=[B["meanu"]])
        bank = B["P6"]
        for m in range(4):
            for k in range(8):
                mm(bank, PS[6][:, m:m + 1],
                   A["mA"][:, 512 * k + 128 * m:512 * k + 128 * (m + 1)],
                   A["meanu"][:, k:k + 1], k == 0, k == 7,
                   reads=[B["mA"][k], B["meanu"]])
        for m in range(4):
            s(lambda e, m=m: e.activation(A["gcol1"][:, m:m + 1], PS[6][:, m:m + 1],
                                          AF.Gelu, bias=A["gb1c"][:, m:m + 1]),
              reads=[bank, B["gb1c"]], writes=[B["gcol1"]])
        bank = B["P7"]
        for m in range(2):
            for k in range(4):
                mm(bank, PS[7][:, m:m + 1],
                   A["mB"][:, 256 * k + 128 * m:256 * k + 128 * (m + 1)],
                   A["gcol1"][:, k:k + 1], k == 0, k == 3,
                   reads=[B["mB"][k // 2], B["gcol1"]])
        for m in range(2):
            s(lambda e, m=m: e.activation(A["gcol2"][:, m:m + 1], PS[7][:, m:m + 1],
                                          AF.Gelu, bias=A["gb2c"][:, m:m + 1]),
              reads=[bank, B["gb2c"]], writes=[B["gcol2"]])
        bank = B["P5"]
        for k in range(2):
            mm(bank, PS[5][0:1, 0:1], A["gc3"][:, k:k + 1], A["gcol2"][:, k:k + 1],
               k == 0, k == 1, reads=[B["gc3"], B["gcol2"]])
        s(lambda e: e.activation(A["gfv"][:], PS[5][0:1, 0:1], AF.Tanh,
                                 scale=0.5, bias=A["hb"][0:1, 1:2]),
          reads=[bank, B["hb"]], writes=[B["gfv"]])
        v(lambda e: e.tensor_scalar(A["kap"][0:1, 1:2], A["arw"][:], 0.3, None, OP.mult),
          reads=[B["arw"]], writes=[B["kap"]])
        v(lambda e: e.tensor_tensor(A["kap"][0:1, 0:1], A["arw"][:], A["gfv"][:], OP.mult),
          reads=[B["arw"], B["gfv"]], writes=[B["kap"]])
        v(lambda e: e.tensor_scalar(A["kap"][0:1, 0:1], A["kap"][0:1, 0:1],
                                    0.7, 0.1, OP.mult, OP.add),
          reads=[], writes=[B["kap"]])
        v(lambda e: e.memset(A["sumrow"][:], 0.0), writes=[B["sumrow"]])

    def sum_u(self, slot, pi=3):
        self.phase = f"sumu.j{slot}"
        A, B, PS = self.A, self.B, self.PS
        bank = B[f"P{pi}"]
        for hh in range(2):
            self.mm(bank, PS[pi][0:1, 0:512], A["oncb"][:],
                    A["u2t"][:, 1024 * slot + 512 * hh:1024 * slot + 512 * (hh + 1)],
                    True, True, reads=[B["oncb"], B["u2t"][slot]])
            self.v(lambda e, hh=hh:
                   e.tensor_add(A["sumrow"][0:1, 512 * hh:512 * (hh + 1)],
                                A["sumrow"][0:1, 512 * hh:512 * (hh + 1)],
                                PS[pi][0:1, 0:512]),
                   reads=[bank], writes=[B["sumrow"]])

    def sum_x(self, t, j, ac, pi=3):
        self.phase = f"sumu.j{j}"
        A, B, PS = self.A, self.B, self.PS
        xwA, xwB = self.xw(t)
        col = (4 * t + j) % 32
        bank = B[f"P{pi}"]
        for hh in range(2):
            self.mm(bank, PS[pi][0:1, 0:512], A[ac][:, col:col + 1],
                    xwA[:, 1024 * j + 512 * hh:1024 * j + 512 * (hh + 1)],
                    True, True, reads=[B[ac], xwB[j]])
            self.v(lambda e, hh=hh:
                   e.tensor_add(A["sumrow"][0:1, 512 * hh:512 * (hh + 1)],
                                A["sumrow"][0:1, 512 * hh:512 * (hh + 1)],
                                PS[pi][0:1, 0:512]),
                   reads=[bank], writes=[B["sumrow"]])

    def sweep1(self):
        self.sweep_tag = "w1"
        self.u1mod = 5
        """Per-chunk LN1 pipeline (stats | statmath | u-build+sum skewed by
        one chunk each) with setup2's serial ladder work fed into the gaps."""
        A, B = self.A, self.B
        uTB = B["u1TM"]
        gen = self.setup2_steps()

        def F():
            ph = self.phase
            next(gen, None)
            self.phase = ph
        self.load_xtile(0)
        if self.NT > 1:
            self.load_xtile(1)
        for _ in range(27):
            F()
        seq = [(t, j) for t in range(self.NT) for j in range(NJ)]
        n = len(seq)
        for idx in range(n + 2):
            if idx < n:
                t, j = seq[idx]
                self.ln_stats_j(t, j)
            F()
            if 1 <= idx and idx - 1 < n:
                t, j = seq[idx - 1]
                self.ln_statmath_j(t, j, "a1c", "bb1c")
            F()
            if 2 <= idx and idx - 2 < n:
                t, j = seq[idx - 2]
                self.sum_x(t, j, "a1c")
                if j == NJ - 1 and t + 2 < self.NT:
                    self.load_xtile(t + 2)
        for _ in gen:
            pass

    def sweep2(self):
        self.sweep_tag = "w2"
        self.u1mod = 5
        """Software-pipelined: FFN of tile t runs while the SSM of tile t-1
        is interleaved into the FFN's TE slots.  FFN: banks P5-P7; SSM:
        banks P0-P4.  Tail (LN3/stores) of t-1 is emitted right after
        ffn_main(t) so the next tile's loads/u-builds/transposes fill the
        coefficient-latency at the FFN boundary."""
        A, B, D = self.A, self.B, self.D
        self.load_xtile(0)
        if self.NT > 1:
            self.load_xtile(1)
        for j in range(NJ):
            self.u_build(0, j, None, None, "a1c", "bb1c")
            self.transpose_uFM_j(0, j, None, None, negate=True)
        self.gf_chain()
        for t in range(self.NT + 1):
            if t < self.NT:
                xwA, xwB = self.xw(t)

                gen = self.ssm_steps(t - 1) if t >= 1 else None
                feeder = (lambda g=gen: next(g, None)) if gen else None
                self.ffn_main(t, None, None, feeder)
                if gen is not None:
                    for _ in gen:
                        pass
            else:
                for _ in self.ssm_steps(t - 1):
                    pass
            if t < self.NT:
                for j in range(NJ):
                    if t + 1 < self.NT:
                        self.u_build(t + 1, j, None, None, "a1c", "bb1c",
                                     eng="v")
                        self.transpose_uFM_j(t + 1, j, None, None, negate=True)
                    self.ffn_apply_j(t, j, None, None)
                    if self.debug:
                        self.dma("q", D["dbg_x1"][TILE * t + 128 * j:TILE * t + 128 * (j + 1), :],
                                 xwA[:, 1024 * j:1024 * (j + 1)], reads=[xwB[j]])
                        self.q_fence()
                    self.ln_stats_j(t, j)
                for j in range(NJ):
                    self.ln_statmath_j(t, j, "a2c", "bb2c")

    def ssm_steps(self, t):
        A, B, PS, D = self.A, self.B, self.PS, self.D
        v, g, mm = self.v, self.g, self.mm
        xwA, xwB = self.xw(t)
        G1 = A["G1s"][:, 0:512]
        G2 = A["G2s"][:, 0:512]
        GB1, GB2 = B["G1s"], B["G2s"]
        h2sl = A["h2s"][:, 0:512]
        h2B = B["h2s"]
        rw, tr, sr, er = A["rw"], A["trow"], A["srow"], A["erow"]
        RW1, RW2 = B["rw"][0], B["rw"][1]
        yield
        yield
        for j in range(NJ):
            if j >= 2:
                self.sum_u((j - 2 + 2) % 3)
            self.u2_prep(t, j)
            yield
            self.phase = f"ssm.t{t}.j{j}.Bu"
            bank0 = B["P0"]
            for k in range(8):
                mm(bank0, PS[0][:, 0:512],
                   A["u2FMt"][:, 128 * k:128 * (k + 1)],
                   A["Bh"][:, 512 * k:512 * (k + 1)], k == 0, k == 7,
                   reads=[B["u2FMt"], B["Bh"]])
            v(lambda e: e.tensor_mul(G1, PS[0][:, 0:512], A["R1"][:]),
              reads=[bank0, B["R1"]], writes=[GB1])
            v(lambda e: e.tensor_mul(G2, PS[0][:, 0:512], A["R2"][:]),
              reads=[bank0, B["R2"]], writes=[GB2])
            yield
            self.phase = f"ssm.t{t}.j{j}.cs"
            b3, b4 = B["P3"], B["P4"]
            mm(b3, PS[3][0:1, 0:512], A["oncf"][:], G1, True, True,
               reads=[B["oncf"], GB1])
            mm(b4, PS[4][0:1, 0:512], A["oncf"][:], G2, True, True,
               reads=[B["oncf"], GB2])
            b1, b2 = B["P1"], B["P2"]
            mm(b1, PS[1][:, 0:512], A["ltri_s"][:], G1, True, False,
               reads=[B["ltri_s"], GB1])
            mm(b1, PS[1][:, 0:512], A["on1f"][:], rw[0:1, 0:512], False, True,
               reads=[B["on1f"], RW1])
            mm(b2, PS[2][:, 0:512], A["ltri_s"][:], G2, True, False,
               reads=[B["ltri_s"], GB2])
            mm(b2, PS[2][:, 0:512], A["on1f"][:], rw[0:1, 512:1024], False, True,
               reads=[B["on1f"], RW2])
            v(lambda e: e.tensor_add(tr[0:1, 0:512], PS[3][0:1, 0:512], rw[0:1, 0:512]),
              reads=[b3, RW1], writes=[B["trow"]])
            v(lambda e: e.tensor_add(tr[0:1, 512:1024], PS[4][0:1, 0:512], rw[0:1, 512:1024]),
              reads=[b4, RW2], writes=[B["trow"]])
            v(lambda e: e.tensor_tensor(rw[0:1, 0:512], tr[0:1, 0:512], er[0:1, 0:512], OP.mult),
              reads=[B["trow"]], writes=[RW1])
            v(lambda e: e.tensor_tensor(sr[0:1, 0:512], tr[0:1, 512:1024],
                                        er[0:1, 512:1024], OP.mult),
              reads=[B["trow"]], writes=[B["srow"]])
            v(lambda e: e.tensor_add(rw[0:1, 0:512], rw[0:1, 0:512], sr[0:1, 0:512]),
              reads=[B["srow"]], writes=[RW1])
            g(lambda e: e.tensor_tensor(rw[0:1, 512:1024], tr[0:1, 0:512],
                                        er[0:1, 1024:1536], OP.mult),
              reads=[B["trow"]], writes=[RW2])
            g(lambda e: e.tensor_tensor(sr[0:1, 512:1024], tr[0:1, 512:1024],
                                        er[0:1, 1536:2048], OP.mult),
              reads=[B["trow"]], writes=[B["srow2"]])
            g(lambda e: e.tensor_add(rw[0:1, 512:1024], rw[0:1, 512:1024],
                                     sr[0:1, 512:1024]),
              reads=[B["srow2"]], writes=[RW2])
            yield
            self.phase = f"ssm.t{t}.j{j}.h2"
            v(lambda e: e.tensor_mul(G1, PS[1][:, 0:512], A["Q21s"][:]),
              reads=[b1, B["Q21s"]], writes=[GB1])
            v(lambda e: e.tensor_mul(G2, PS[2][:, 0:512], A["Q22s"][:]),
              reads=[b2, B["Q22s"]], writes=[GB2])
            v(lambda e: e.tensor_add(h2sl, G1, G2), reads=[GB1, GB2], writes=[h2B])
            b5 = B["P4"]
            for qq in range(4):
                mm(b5, PS[4][:, 128 * qq:128 * (qq + 1)],
                   A["h2s"][:, 128 * qq:128 * (qq + 1)],
                   A["idb"][:], qq == 0, qq == 3, reads=[h2B, B["idb"]])
            for qq in range(4):
                if qq % 2 == 0:
                    self.v(lambda e, qq=qq, j=j:
                           e.tensor_copy(A["ysT"][:, 512 * qq + 128 * j:512 * qq + 128 * (j + 1)],
                                         PS[4][:, 128 * qq:128 * (qq + 1)]),
                           reads=[b5], writes=[B["ysT"]])
                else:
                    self.s(lambda e, qq=qq, j=j:
                           e.copy(A["ysT"][:, 512 * qq + 128 * j:512 * qq + 128 * (j + 1)],
                                  PS[4][:, 128 * qq:128 * (qq + 1)]),
                           reads=[b5], writes=[B["ysT"]])
            yield
            self.phase = f"ssm.t{t}.j{j}.Cy"
            for hh in range(2):
                bank = B["P3"]
                for qq in range(4):
                    mm(bank, PS[3][:, 0:512],
                       A["ysT"][:, 512 * qq + 128 * j:512 * qq + 128 * (j + 1)],
                       A["Ch"][:, 1024 * qq + 512 * hh:1024 * qq + 512 * (hh + 1)],
                       qq == 0, qq == 3, reads=[B["ysT"], B["Ch"]])
                g(lambda e, hh=hh, j=j:
                  e.tensor_mul(A["stile"][:, 512 * hh:512 * (hh + 1)],
                               A["u2t"][:, 1024 * (j % 3) + 512 * hh:
                                         1024 * (j % 3) + 512 * (hh + 1)],
                               A["Dbc"][:, 512 * hh:512 * (hh + 1)]),
                  reads=[B["u2t"][j % 3], B["Dbc"]], writes=[B["stile"][hh]])
                v(lambda e, hh=hh, j=j:
                  e.tensor_add(xwA[:, 1024 * j + 512 * hh:1024 * j + 512 * (hh + 1)],
                               xwA[:, 1024 * j + 512 * hh:1024 * j + 512 * (hh + 1)],
                               PS[3][:, 0:512]),
                  reads=[bank], writes=[xwB[j]])
                v(lambda e, hh=hh, j=j:
                  e.tensor_add(xwA[:, 1024 * j + 512 * hh:1024 * j + 512 * (hh + 1)],
                               xwA[:, 1024 * j + 512 * hh:1024 * j + 512 * (hh + 1)],
                               A["stile"][:, 512 * hh:512 * (hh + 1)]),
                  reads=[B["stile"][hh]], writes=[xwB[j]])
                yield
            self.ln_stats_j(t, j)
            self.ln_statmath_j(t, j, "a3c", "bb3c")
            self.dma("q", D["x2scr"][TILE * t + 128 * j:TILE * t + 128 * (j + 1), :],
                     xwA[:, 1024 * j:1024 * (j + 1)], reads=[xwB[j]])
            if self.debug:
                self.dma("q", D["dbg_x2"][TILE * t + 128 * j:TILE * t + 128 * (j + 1), :],
                         xwA[:, 1024 * j:1024 * (j + 1)], reads=[xwB[j]])
            self.u_build(t, j, A["u2t"], B["u2t"], "a3c", "bb3c",
                         dst_j=(j + 2) % 3)
            yield
        self.sum_u((NJ - 2 + 2) % 3)
        self.sum_u((NJ - 1 + 2) % 3)
        self.q_fence()
        if t + 2 < self.NT:
            self.load_xtile(t + 2)

    def w3_steps(self, tn):
        for _ in range(14):
            yield
        for j in range(NJ):
            self.u_build(tn, j, None, None, "a3c", "bb3c")
            yield
            self.transpose_uFM_j(tn, j, None, None, negate=True)
            yield

    def sweep3(self):
        self.sweep_tag = "w3"
        self.u1mod = 8
        self.tpu_banks = (0, 1)
        A, B, D = self.A, self.B, self.D
        self.load_xtile(0, src="x2scr")
        if self.NT > 1:
            self.load_xtile(1, src="x2scr")
        for j in range(NJ):
            self.u_build(0, j, None, None, "a3c", "bb3c")
            self.transpose_uFM_j(0, j, None, None, negate=True)
        self.gf_chain(None)
        for t in range(self.NT):
            xwA, xwB = self.xw(t)
            gen = self.w3_steps(t + 1) if t + 1 < self.NT else None
            feeder = (lambda g=gen: next(g, None)) if gen else None
            self.ffn_main(t, None, None, feeder)
            if gen is not None:
                for _ in gen:
                    pass
            for j in range(NJ):
                self.ffn_apply_j(t, j, None, None)
                self.dma("q", D["y"][TILE * t + 128 * j:TILE * t + 128 * (j + 1), :],
                         xwA[:, 1024 * j:1024 * (j + 1)], reads=[xwB[j]])
            self.q_fence()
            if t + 2 < self.NT:
                self.load_xtile(t + 2, src="x2scr")


def _make_consts(L):
    bf = ml_dtypes.bfloat16
    return {
        "ltri": np.triu(np.ones((128, 128), np.float32)),
        "identf": np.eye(128, dtype=np.float32),
        "identb": np.eye(128).astype(bf),
        "onescol_f": np.ones((128, 1), np.float32),
        "onescol_b": np.ones((128, 1)).astype(bf),
        "onesrow_b": np.ones((1, 128)).astype(bf),
        "onesrow_f": np.ones((1, 128), np.float32),
        "consts": np.array([[0.5, -0.5, 1.0, 1.0 / L]], np.float32),
    }


def make_inmap(inputs, core, L=4096):
    asc = lambda a: np.ascontiguousarray(np.asarray(a, dtype=np.float32))
    x = np.asarray(inputs["L0_attn_input_sequence"])
    m = {
        "x0": asc(x[core][:L]),
        "cn_w1": asc(inputs["L0_ffn_cn_w1"]), "cn_b1": asc(inputs["L0_ffn_cn_b1"]),
        "cn_w2": asc(inputs["L0_ffn_cn_w2"]), "cn_b2": asc(inputs["L0_ffn_cn_b2"]),
        "cn_w3": asc(inputs["L0_ffn_cn_w3"]), "cn_b3": asc(inputs["L0_ffn_cn_b3"]),
        "cn_w4": asc(inputs["L0_ffn_cn_w4"]), "cn_b4": asc(inputs["L0_ffn_cn_b4"]),
        "gc_w1": asc(inputs["L0_ffn_gc_w1"]), "gc_b1": asc(inputs["L0_ffn_gc_b1"]),
        "gc_w2": asc(inputs["L0_ffn_gc_w2"]), "gc_b2": asc(inputs["L0_ffn_gc_b2"]),
        "gc_w3": asc(inputs["L0_ffn_gc_w3"]), "gc_b3": asc(inputs["L0_ffn_gc_b3"]),
        "ar": asc(inputs["L0_ffn_adaptive_range"]),
        "steps_raw": asc(inputs["L0_attn_steps_raw"]),
        "G_raw": asc(inputs["L0_attn_G_raw"]),
        "A_raw": asc(inputs["L0_attn_A_raw"]),
        "B2": asc(np.asarray(inputs["L0_attn_B_mat"]).reshape(256, 2048)),
        "C2": asc(np.asarray(inputs["L0_attn_C_mat"]).reshape(1024, 512)),
        "D_vec": asc(inputs["L0_attn_D_vec"]),
    }
    m.update(_make_consts(L))
    return m


def kernel(**inputs):
    kb = KB(n_tiles=8, debug=False)
    nc = kb.build()
    in_maps = [make_inmap(inputs, c) for c in range(8)]
    res = run_bass_kernel_spmd(nc, in_maps, core_ids=list(range(8)))
    return np.stack([np.asarray(r["y"]) for r in res.results], axis=0)


# version 23
# speedup vs baseline: 1.0261x; 1.0261x over previous
import numpy as np
import ml_dtypes
import concourse.bass as bass
import concourse.mybir as mybir
from concourse.bass_utils import run_bass_kernel_spmd

F32 = mybir.dt.float32
I32 = mybir.dt.int32
BF16 = mybir.dt.bfloat16
AF = mybir.ActivationFunctionType
OP = mybir.AluOpType
AX = mybir.AxisListType

H = 1024
TILE = 512
NJ = 4
EPS = 1e-5
ENGINES = ("sync", "tensor", "vector", "scalar", "gpsimd")
SIGMA = [2, 3, 0, 1]
SIGSGN = [+1, -1, -1, +1]
PERM = [4, 5, 6, 7, 0, 1, 2, 3]
PSGN = [+1, +1, -1, -1, -1, -1, +1, +1]


class Buf:
    __slots__ = ("w", "rs", "frozen", "pend_r", "pend_w", "pend_q")

    def __init__(self):
        self.w = None
        self.rs = []
        self.frozen = False
        self.pend_r = False
        self.pend_w = False
        self.pend_q = None

    def rd(self, tok):
        if not self.frozen and tok is not None:
            self.rs.append(tok)

    def rw(self):
        assert not self.pend_w, "read of DMA-write-pending buffer before fence"
        return [self.w] if self.w is not None else []

    def ww(self):
        assert not (self.pend_w or self.pend_r), \
            "write of DMA-pending buffer before fence"
        ws = list(self.rs)
        if self.w is not None:
            ws.append(self.w)
        return ws

    def wrote(self, tok):
        if not self.frozen:
            self.w = tok
            self.rs = []

    def freeze(self):
        self.frozen = True
        self.rs = []


class KB:
    def __init__(self, n_tiles=8, debug=False):
        self.NT = n_tiles
        self.L = TILE * n_tiles
        self.debug = debug
        nc = bass.Bass("TRN2", target_bir_lowering=False)
        self.nc = nc
        self.q = {e: [] for e in ENGINES}
        self.h = {}
        self.cnt = {}
        self._gd_pending = []
        self._q_pending = []
        self._ctx = []
        for n in ("T", "V", "S", "G", "Q", "GD", "QF"):
            g = nc.semaphore(n)
            self.h[n] = g.__enter__()
            self._ctx.append(g)
            self.cnt[n] = 0
        self._declare()
        self.phase = "init"
        self.te_labels = []

    def _sched(self, eng, fn, waits, sem, incn=1):
        self.cnt[sem] += incn
        tok = (sem, self.cnt[sem], eng)
        wd = {}
        for w in waits:
            if w is None:
                continue
            s, v, weng = w
            if eng == "tensor" and weng == "tensor":
                continue
            assert v <= self.cnt[s], f"future wait {s}:{v}"
            if wd.get(s, 0) < v:
                wd[s] = v
        self.q[eng].append((tuple(wd.items()), fn, (sem, incn)))
        return tok

    def step(self, eng, fn, reads=(), writes=(), extra=()):
        waits = list(extra)
        for b in reads:
            waits += b.rw()
        for b in writes:
            waits += b.ww()
        sem = {"sync": "Q", "tensor": "T", "vector": "V",
               "scalar": "S", "gpsimd": "G"}[eng]
        tok = self._sched(eng, fn, waits, sem)
        for b in reads:
            b.rd(tok)
        for b in writes:
            b.wrote(tok)
        return tok

    def dma(self, qname, out, in_, reads=(), writes=()):
        waits = []
        for b in reads:
            if b.pend_q == qname:
                continue
            waits += b.rw()
        for b in writes:
            if b.pend_q == qname:
                continue
            waits += b.ww()
        eng, sem = (("sync", "Q") if qname == "q" else ("gpsimd", "GD"))
        waits = [w for w in waits if w is None or w[0] != sem]
        tok = self._sched(eng, lambda e, o=out, i=in_: e.dma_start(out=o, in_=i),
                          waits, sem, 16)
        pend = self._gd_pending if qname == "g" else self._q_pending
        for b in reads:
            assert b.pend_q in (None, qname), "buffer pending on other queue"
            pend.append((b, "r"))
            b.pend_r = True
            b.pend_q = qname
        for b in writes:
            assert b.pend_q in (None, qname), "buffer pending on other queue"
            pend.append((b, "w"))
            b.pend_w = True
            b.pend_q = qname
        return tok

    def _fence(self, pend, eng, dmasem, fsem):
        if not pend:
            return None
        ftok = self._sched(eng, lambda e: e.nop(),
                           [(dmasem, self.cnt[dmasem], "dma")], fsem)
        for b, kind in pend:
            b.pend_r = False
            b.pend_w = False
            b.pend_q = None
        for b, kind in pend:
            if kind == "w":
                b.wrote(ftok)
        for b, kind in pend:
            if kind == "r":
                b.rd(ftok)
        del pend[:]
        return ftok

    def g_fence(self):
        return self._fence(self._gd_pending, "gpsimd", "GD", "G")

    def q_fence(self):
        return self._fence(self._q_pending, "sync", "Q", "QF")

    def mm(self, bank, psum_ap, lhsT, rhs, start, stop, reads=()):
        self.te_labels.append(self.phase)
        waits = []
        for b in reads:
            waits += b.rw()
        waits += bank.ww() if start else bank.rw()
        tok = self._sched("tensor",
                          lambda e, ps=psum_ap, l=lhsT, r=rhs, st=start, sp=stop:
                          e.matmul(ps, l, r, start=st, stop=sp), waits, "T")
        for b in reads:
            b.rd(tok)
        bank.wrote(tok)
        return tok

    def v(self, fn, reads=(), writes=(), extra=()):
        return self.step("vector", fn, reads, writes, extra)

    def s(self, fn, reads=(), writes=(), extra=()):
        return self.step("scalar", fn, reads, writes, extra)

    def g(self, fn, reads=(), writes=(), extra=()):
        return self.step("gpsimd", fn, reads, writes, extra)

    def emit(self, blk):
        for e in ENGINES:
            if not self.q[e]:
                continue
            queue = self.q[e]

            def body(eng, queue=queue):
                seen = {}
                for waits, fn, inc in queue:
                    for (s, v) in waits:
                        if seen.get(s, 0) >= v:
                            continue
                        eng.wait_ge(self.h[s], v)
                        seen[s] = v
                    ins = fn(eng)
                    if inc is not None:
                        ins.then_inc(self.h[inc[0]], inc[1])
            getattr(blk, e)(body)
        self.q = {e: [] for e in ENGINES}

    def _declare(self):
        nc, L = self.nc, self.L
        D = {}
        def din(name, shape, dt=F32):
            D[name] = nc.dram_tensor(name, shape, dt, kind="ExternalInput").ap()
        din("x0", [L, H])
        din("cn_w1", [H, H]); din("cn_b1", [H])
        din("cn_w2", [H, 512]); din("cn_b2", [512])
        din("cn_w3", [512, 256]); din("cn_b3", [256])
        din("cn_w4", [256, 1]); din("cn_b4", [1])
        din("gc_w1", [H, 512]); din("gc_b1", [512])
        din("gc_w2", [512, 256]); din("gc_b2", [256])
        din("gc_w3", [256, 1]); din("gc_b3", [1])
        din("ar", [1])
        din("steps_raw", [256]); din("G_raw", [256]); din("A_raw", [256])
        din("B2", [256, 2 * H]); din("C2", [H, 512]); din("D_vec", [H])
        din("ltri", [128, 128]); din("identf", [128, 128])
        din("identb", [128, 128], BF16)
        din("onescol_f", [128, 1]); din("onescol_b", [128, 1], BF16)
        din("onesrow_b", [1, 128], BF16); din("onesrow_f", [1, 128])
        din("consts", [1, 4])
        D["y"] = nc.dram_tensor("y", [L, H], F32, kind="ExternalOutput").ap()
        D["x2scr"] = nc.dram_tensor("x2scr", [L, H], F32, kind="Internal").ap()
        if self.debug:
            for nm in ("dbg_x1", "dbg_x2", "dbg_u1"):
                D[nm] = nc.dram_tensor(nm, [L, H], F32, kind="ExternalOutput").ap()
        self.D = D

        A = {}
        def sb(name, shape, dt):
            g = nc.sbuf_tensor(name, shape, dt)
            A[name] = g.__enter__()
            self._ctx.append(g)
        sb("xw0", [128, 4096], F32); sb("xw1", [128, 4096], F32)
        sb("u1TM", [128, 4096], BF16); sb("u2t", [128, 3072], BF16)
        sb("u2FMt", [128, 1024], BF16)
        sb("uFM", [128, 4096], BF16); sb("uFMn", [128, 2048], BF16)
        sb("mA", [128, 4096], BF16); sb("mB", [128, 4096], BF16)
        sb("h1", [128, 4096], BF16); sb("h2", [128, 2048], BF16)
        sb("h3", [128, 1024], BF16); sb("h1p", [128, 1024], BF16)
        sb("stile", [128, 1024], BF16)
        sb("w1s", [128, 8192], BF16)
        sb("w2s", [128, 4096], BF16); sb("w3s", [128, 1024], BF16)
        sb("w4s", [128, 2], BF16)
        sb("gc3", [128, 2], BF16)
        sb("Bh", [128, 4096], BF16); sb("Ch", [128, 4096], BF16)
        sb("Dbc", [128, 1024], BF16)
        sb("b1c", [128, 8], F32); sb("b2c", [128, 4], F32)
        sb("b3c", [128, 2], F32); sb("b4c", [1, 1], F32)
        sb("gb1c", [128, 4], F32); sb("gb2c", [128, 2], F32)
        sb("gb3c", [1, 1], F32); sb("arw", [1, 1], F32)
        sb("R1", [128, 512], F32); sb("R2", [128, 512], F32)
        sb("Q21s", [128, 512], F32); sb("Q22s", [128, 512], F32)
        sb("G1s", [128, 512], F32); sb("G2s", [128, 512], F32)
        sb("h2s", [128, 512], BF16)
        sb("ysT", [128, 2048], BF16)
        sb("ltri_s", [128, 128], F32); sb("idf", [128, 128], F32)
        sb("idb", [128, 128], BF16)
        sb("oncf", [128, 1], F32); sb("oncb", [128, 1], BF16)
        sb("on1b", [1, 128], BF16); sb("on1f", [1, 128], F32)
        sb("cst", [1, 4], F32)
        sb("rw", [1, 1024], F32)
        sb("erow", [1, 2048], F32)
        sb("trow", [1, 1024], F32)
        sb("srow", [1, 1024], F32)
        sb("crow", [1, 2048], F32)
        sb("abrow", [1, 1024], F32)
        sb("sumrow", [1, 1024], F32)
        sb("brow", [1, 2048], BF16)
        sb("statS", [128, 4], F32); sb("statQ", [128, 4], F32)
        sb("a1c", [128, 32], F32); sb("bb1c", [128, 32], F32)
        sb("a2c", [128, 32], F32); sb("bb2c", [128, 32], F32)
        sb("a3c", [128, 32], F32); sb("bb3c", [128, 32], F32)
        sb("stt1", [128, 8], F32); sb("stt2", [128, 8], F32)
        sb("stt3", [128, 8], F32)
        sb("meanu", [128, 8], BF16)
        sb("gcol1", [128, 4], BF16); sb("gcol2", [128, 2], BF16)
        sb("gfv", [1, 1], F32); sb("kap", [1, 2], F32); sb("hb", [1, 2], F32)
        sb("bbsum", [1, 1], F32)
        sb("stt4", [128, 8], F32)
        sb("abc", [128, 12], F32)
        sb("cbcA", [128, 512], BF16); sb("cbcB", [128, 512], BF16)
        sb("PT11", [128, 256], F32); sb("PT12", [128, 256], F32)
        sb("PT21", [128, 256], F32); sb("PT22", [128, 256], F32)
        sb("RP1", [128, 256], F32); sb("RP2", [128, 256], F32)
        sb("lad", [128, 16], F32); sb("ladQ", [128, 16], F32)
        sb("mcols", [128, 24], F32)
        self.A = A
        self.PS = []
        for i in range(8):
            g = nc.psum_tensor(f"P{i}", [128, 512], F32)
            self.PS.append(g.__enter__())
            self._ctx.append(g)
        names = list(A) + [f"P{i}" for i in range(8)] + ["xwA", "xwB"]
        for drop in ("xw0", "xw1"):
            names.remove(drop)
        self.B = {k: Buf() for k in names}
        SPLIT = dict(mA=8, mB=8, h1=8, h2=4, h3=2, uFM=8, stile=2, rw=2)
        COARSE = set("uFM".split())
        for nm, n in SPLIT.items():
            if nm in COARSE:
                b = Buf()
                self.B[nm] = [b] * n
            else:
                self.B[nm] = [Buf() for _ in range(n)]
        for nm in ("xwA", "xwB"):
            if nm in COARSE:
                b = Buf()
                self.B[nm] = [b] * NJ
            else:
                self.B[nm] = [Buf() for _ in range(NJ)]
        self.B["srow2"] = Buf()
        self.B["u2t"] = [Buf() for _ in range(3)]
        self.B["u1TM"] = [Buf() for _ in range(NJ)]
        for nm in ("stt1", "stt2", "stt3", "stt4", "statS", "statQ"):
            self.B[nm] = [Buf() for _ in range(NJ)]

    def build(self):
        self.setup()
        self.sweep1()
        self.sweep2()
        self.sweep3()
        self.q["sync"].append(
            (((("Q", self.cnt["Q"]), ("GD", self.cnt["GD"]))),
             lambda e: e.nop(), None))
        with self.nc.Block() as blk:
            self.emit(blk)
        return self.nc

    def setup(self):
        self.phase = "setup"
        A, B, D, PS = self.A, self.B, self.D, self.PS
        dma, v, s, mm = self.dma, self.v, self.s, self.mm
        xw0, xw1 = A["xw0"], A["xw1"]
        XA, XB = B["xwA"], B["xwB"]

        for nm, dst in [("consts", "cst"), ("ltri", "ltri_s"), ("identf", "idf"),
                        ("onescol_f", "oncf"), ("onesrow_f", "on1f"),
                        ("ar", "arw"), ("cn_b4", "b4c"), ("gc_b3", "gb3c"),
                        ("identb", "idb"), ("onescol_b", "oncb"),
                        ("onesrow_b", "on1b")]:
            dma("q", A[dst][:], D[nm], writes=[B[dst]])
        for c in range(8):
            dma("q", A["b1c"][:, c:c + 1], D["cn_b1"][128 * c:128 * (c + 1)],
                writes=[B["b1c"]])
        for c in range(4):
            dma("q", A["b2c"][:, c:c + 1], D["cn_b2"][128 * c:128 * (c + 1)],
                writes=[B["b2c"]])
            dma("q", A["gb1c"][:, c:c + 1], D["gc_b1"][128 * c:128 * (c + 1)],
                writes=[B["gb1c"]])
        for c in range(2):
            dma("q", A["b3c"][:, c:c + 1], D["cn_b3"][128 * c:128 * (c + 1)],
                writes=[B["b3c"]])
            dma("q", A["gb2c"][:, c:c + 1], D["gc_b2"][128 * c:128 * (c + 1)],
                writes=[B["gb2c"]])
        SL = {}
        for i, nm in enumerate(["sri", "gri", "ari", "ST", "GG", "AA", "SS",
                                "SI", "AL", "RA", "RB", "RC", "RD", "STI",
                                "F1C", "F2C"]):
            SL[nm] = xw0[0:1, 256 * i:256 * (i + 1)]
        for i, nm in enumerate(["MA", "MB", "MC", "MD"]):
            SL[nm] = xw1[0:1, 256 * i:256 * (i + 1)]
        SL["Drow"] = A["srow"][0:1, 0:1024]
        self._SL = SL
        dma("q", SL["sri"], D["steps_raw"], writes=[*XA])
        dma("q", SL["gri"], D["G_raw"], writes=[*XA])
        dma("q", SL["ari"], D["A_raw"], writes=[*XA])
        dma("q", SL["Drow"], D["D_vec"], writes=[B["srow"]])
        self.q_fence()
        v(lambda e: e.tensor_scalar(A["hb"][0:1, 0:1], A["b4c"][:], 0.5, None, OP.mult),
          reads=[B["b4c"]], writes=[B["hb"]])
        v(lambda e: e.tensor_scalar(A["hb"][0:1, 1:2], A["gb3c"][:], 0.5, None, OP.mult),
          reads=[B["gb3c"]], writes=[B["hb"]])
        s(lambda e: e.activation(SL["ST"], SL["sri"], AF.Tanh, scale=0.5),
          reads=[*XA], writes=[*XA])
        v(lambda e: e.tensor_scalar(SL["ST"], SL["ST"], 0.5, 0.5, OP.mult, OP.add),
          reads=[*XA], writes=[*XA])
        v(lambda e: e.tensor_scalar(SL["GG"], SL["gri"], 0.0, None, OP.max),
          reads=[*XA], writes=[*XA])
        v(lambda e: e.tensor_scalar(SL["AA"], SL["ari"], 0.0, None, OP.max),
          reads=[*XA], writes=[*XA])
        v(lambda e: e.tensor_tensor(SL["SS"], SL["ST"], SL["GG"], OP.mult),
          reads=[*XA], writes=[*XA])
        v(lambda e: e.tensor_scalar(SL["SS"], SL["SS"], 1.0, None, OP.add),
          reads=[*XA], writes=[*XA])
        v(lambda e: e.reciprocal(SL["SI"], SL["SS"]), reads=[*XA], writes=[*XA])
        v(lambda e: e.tensor_tensor(SL["AL"], SL["ST"], SL["ST"], OP.mult),
          reads=[*XA], writes=[*XA])
        v(lambda e: e.tensor_tensor(SL["AL"], SL["AL"], SL["AA"], OP.mult),
          reads=[*XA], writes=[*XA])
        v(lambda e: e.tensor_tensor(SL["AL"], SL["AL"], SL["SI"], OP.mult),
          reads=[*XA], writes=[*XA])
        s(lambda e: e.activation(SL["AL"], SL["AL"], AF.Tanh, scale=1.0 / 1.99),
          reads=[*XA], writes=[*XA])
        v(lambda e: e.tensor_scalar(SL["AL"], SL["AL"], 1.99, None, OP.mult),
          reads=[*XA], writes=[*XA])
        v(lambda e: e.tensor_copy(SL["RA"], SL["SI"]), reads=[*XA], writes=[*XA])
        v(lambda e: e.reciprocal(SL["STI"], SL["ST"]), reads=[*XA], writes=[*XA])
        v(lambda e: e.tensor_tensor(SL["RB"], SL["AL"], SL["STI"], OP.mult),
          reads=[*XA], writes=[*XA])
        v(lambda e: e.tensor_scalar(SL["RB"], SL["RB"], -1.0, None, OP.mult),
          reads=[*XA], writes=[*XA])
        v(lambda e: e.tensor_tensor(SL["RC"], SL["ST"], SL["SI"], OP.mult),
          reads=[*XA], writes=[*XA])
        v(lambda e: e.tensor_scalar(SL["RD"], SL["AL"], -1.0, 1.0, OP.mult, OP.add),
          reads=[*XA], writes=[*XA])
        v(lambda e: e.tensor_copy(SL["F1C"], SL["RC"]), reads=[*XA], writes=[*XA])
        v(lambda e: e.tensor_tensor(SL["F2C"], SL["ST"], SL["RC"], OP.mult),
          reads=[*XA], writes=[*XA])
        v(lambda e: e.tensor_tensor(SL["MA"], SL["RD"], SL["SS"], OP.mult),
          reads=[*XA], writes=[*XB])
        v(lambda e: e.tensor_tensor(SL["MB"], SL["RB"], SL["SS"], OP.mult),
          reads=[*XA], writes=[*XB])
        v(lambda e: e.tensor_scalar(SL["MB"], SL["MB"], -1.0, None, OP.mult),
          reads=[*XB], writes=[*XB])
        v(lambda e: e.tensor_tensor(SL["MC"], SL["RC"], SL["SS"], OP.mult),
          reads=[*XA], writes=[*XB])
        v(lambda e: e.tensor_scalar(SL["MC"], SL["MC"], -1.0, None, OP.mult),
          reads=[*XB], writes=[*XB])
        v(lambda e: e.tensor_tensor(SL["MD"], SL["RA"], SL["SS"], OP.mult),
          reads=[*XA], writes=[*XB])

        order = ["MA", "MB", "MC", "MD", "RA", "RB", "RC", "RD", "F1C", "F2C"]
        bank = B["P0"]
        for idx, nm in enumerate(order):
            sl = SL[nm]
            for ch in range(2):
                self.mm(bank, PS[0][:, 2 * idx + ch:2 * idx + ch + 1],
                        sl[0:1, 128 * ch:128 * (ch + 1)],
                        A["cst"][0:1, 2:3], True, True,
                        reads=[*XA, *XB, B["cst"]])
        v(lambda e: e.tensor_copy(A["mcols"][:, 0:20], PS[0][:, 0:20]),
          reads=[bank], writes=[B["mcols"]])
        MCI = {"ai": 0, "bi": 2, "ci": 4, "di": 6, "a": 8, "b": 10,
               "c": 12, "d": 14, "f1c": 16, "f2c": 18}
        def mc(nm, ch):
            return A["mcols"][:, MCI[nm] + ch:MCI[nm] + ch + 1]

        self._mc = mc


        for k in range(8):
            dma("g", A["w1s"][:, 1024 * k:1024 * (k + 1)],
                D["cn_w1"][128 * k:128 * (k + 1), :], writes=[B["w1s"]])
            dma("g", A["w2s"][:, 512 * k:512 * (k + 1)],
                D["cn_w2"][128 * k:128 * (k + 1), :], writes=[B["w2s"]])
        for k in range(4):
            dma("g", A["w3s"][:, 256 * k:256 * (k + 1)],
                D["cn_w3"][128 * k:128 * (k + 1), :], writes=[B["w3s"]])
        for k in range(2):
            dma("g", A["w4s"][:, k:k + 1], D["cn_w4"][128 * k:128 * (k + 1), :],
                writes=[B["w4s"]])
            dma("g", A["gc3"][:, k:k + 1], D["gc_w3"][128 * k:128 * (k + 1), :],
                writes=[B["gc3"]])
        for pc in range(2):
            dma("g", A["mA"][:, 2048 * pc:2048 * (pc + 1)],
                D["B2"][128 * pc:128 * (pc + 1), :], writes=[*B["mA"]])
        for hk in range(8):
            dma("g", A["mB"][:, 512 * hk:512 * (hk + 1)],
                D["C2"][128 * hk:128 * (hk + 1), :], writes=[*B["mB"]])
        self.g_fence()

        v(lambda e: e.memset(A["rw"][:], 0.0), writes=[*B["rw"]])
        v(lambda e: e.memset(A["sumrow"][:], 0.0), writes=[B["sumrow"]])

        for k in ("w1s", "w2s", "w3s", "w4s", "gc3",
                  "Bh", "Ch", "Dbc", "b1c", "b2c", "b3c", "b4c", "gb1c",
                  "gb2c", "gb3c", "arw", "R1", "R2", "Q21s", "Q22s",
                  "ltri_s", "idf", "idb", "oncf", "oncb", "on1b", "on1f",
                  "cst", "erow", "mcols"):
            self.B[k].freeze()

    def setup2_steps(self):
        self.phase = "setup2"
        """Scan-tile construction etc. -- emitted after sweep1 so its long
        serial VE chains overlap sweep-2's head."""
        A, B, D, PS = self.A, self.B, self.D, self.PS
        dma, v, s, mm = self.dma, self.v, self.s, self.mm
        xw0, xw1 = A["xw0"], A["xw1"]
        XA, XB = B["xwA"], B["xwB"]
        SL = self._SL
        mc = self._mc
        for pc in range(2):
            for hk in range(8):
                yield
                for t in range(2):
                    st_col = 2048 * pc + 256 * hk + t
                    bank = B["P3"]
                    self.mm(bank, PS[3][:, 0:128],
                            A["mA"][:, st_col:st_col + 255:2], A["idb"][:],
                            True, True, reads=[*B["mA"], B["idb"]])
                    v(lambda e, hk=hk, t=t, pc=pc:
                      e.tensor_copy(A["Bh"][:, 512 * hk + 256 * t + 128 * pc:
                                            512 * hk + 256 * t + 128 * (pc + 1)],
                                    PS[3][:, 0:128]),
                      reads=[bank], writes=[B["Bh"]])
        for hk in range(8):
            yield
            for qq in range(4):
                t, pc = qq // 2, qq % 2
                st_col = 512 * hk + 256 * pc + t
                bank = B["P4"]
                self.mm(bank, PS[4][:, 0:128],
                        A["mB"][:, st_col:st_col + 255:2], A["idb"][:],
                        True, True, reads=[*B["mB"], B["idb"]])
                v(lambda e, hk=hk, qq=qq, t=t:
                  e.tensor_scalar(A["Ch"][:, 1024 * qq + 128 * hk:1024 * qq + 128 * (hk + 1)],
                                  PS[4][:, 0:128], 1.0 if t == 0 else -1.0,
                                  None, OP.mult),
                  reads=[bank], writes=[B["Ch"]])

        v(lambda e: e.tensor_copy(A["brow"][0:1, 1536:2048], SL["Drow"][0:1, 0:512]),
          reads=[B["srow"]], writes=[B["brow"]])
        v(lambda e: e.tensor_copy(A["brow"][0:1, 1024:1536], SL["Drow"][0:1, 512:1024]),
          reads=[B["srow"]], writes=[B["brow"]])
        for hh in range(2):
            bank = B["P5"]
            self.mm(bank, PS[5][:, 0:512], A["on1b"][:],
                    A["brow"][0:1, (1536 if hh == 0 else 1024):
                              (2048 if hh == 0 else 1536)],
                    True, True, reads=[B["on1b"], B["brow"]])
            v(lambda e, hh=hh:
              e.tensor_copy(A["Dbc"][:, 512 * hh:512 * (hh + 1)], PS[5][:, 0:512]),
              reads=[bank], writes=[B["Dbc"]])


        def build_powers(tiles, ladname, e11, e12, e21, e22, bottom_only):
            lt = A[ladname]
            LB = B[ladname]
            for i, nm in enumerate((e11, e12, e21, e22)):
                for ch in range(2):
                    v(lambda e, i=i, ch=ch, nm=nm:
                      e.tensor_copy(lt[:, 2 * i + ch:2 * i + ch + 1], mc(nm, ch)),
                      reads=[B["mcols"]], writes=[LB])
            use = ["11", "12", "21", "22"] if not bottom_only else ["21", "22"]
            for nm in use:
                T = tiles[nm]
                for ch in range(2):
                    val = 1.0 if nm in ("11", "22") else 0.0
                    v(lambda e, T=T, ch=ch, val=val:
                      e.memset(A[T][:, 128 * ch:128 * ch + 1], val),
                      writes=[B[T]])
            yield
            d, par = 1, 0
            while d < 128:
                for ch in range(2):
                    o = 128 * ch
                    for nm, (sa, sb_, ia, ib) in {
                        "11": ("11", "12", 0, 2), "12": ("11", "12", 1, 3),
                        "21": ("21", "22", 0, 2), "22": ("21", "22", 1, 3),
                    }.items():
                        if bottom_only and nm not in ("21", "22"):
                            continue
                        TO, TA_, TB_ = tiles[nm], tiles[sa], tiles[sb_]
                        la = lt[:, 8 * par + 2 * ia:8 * par + 2 * ia + 1] \
                            if ch == 0 else lt[:, 8 * par + 2 * ia + 1:8 * par + 2 * ia + 2]
                        lb = lt[:, 8 * par + 2 * ib:8 * par + 2 * ib + 1] \
                            if ch == 0 else lt[:, 8 * par + 2 * ib + 1:8 * par + 2 * ib + 2]
                        v(lambda e, TO=TO, TA_=TA_, la=la, o=o, d=d:
                          e.tensor_scalar(A[TO][:, o + d:o + 2 * d],
                                          A[TA_][:, o:o + d], la, None, OP.mult),
                          reads=[B[TA_], LB], writes=[B[TO]])
                        v(lambda e, TO=TO, TB_=TB_, lb=lb, o=o, d=d:
                          e.scalar_tensor_tensor(A[TO][:, o + d:o + 2 * d],
                                                 A[TB_][:, o:o + d], lb,
                                                 A[TO][:, o + d:o + 2 * d],
                                                 OP.mult, OP.add),
                          reads=[B[TB_], LB], writes=[B[TO]])
                np_ = 1 - par
                for ch in range(2):
                    li = lambda i, ch=ch, par=par: lt[:, 8 * par + 2 * i + ch:8 * par + 2 * i + ch + 1]
                    lo = lambda i, ch=ch, np_=np_: lt[:, 8 * np_ + 2 * i + ch:8 * np_ + 2 * i + ch + 1]
                    for oi, (x1_, y1, x2_, y2) in enumerate(
                            [(0, 0, 1, 2), (0, 1, 1, 3), (2, 0, 3, 2), (2, 1, 3, 3)]):
                        v(lambda e, lo=lo, li=li, oi=oi, x1_=x1_, y1=y1:
                          e.tensor_tensor(lo(oi), li(x1_), li(y1), OP.mult),
                          reads=[LB], writes=[LB])
                        v(lambda e, lo=lo, li=li, oi=oi, x2_=x2_, y2=y2:
                          e.scalar_tensor_tensor(lo(oi), li(x2_), li(y2), lo(oi),
                                                 OP.mult, OP.add),
                          reads=[LB], writes=[LB])
                par = np_
                d *= 2
                yield
            return par

        parP = yield from build_powers({"11": "PT11", "12": "PT12", "21": "PT21",
                             "22": "PT22"}, "lad", "ai", "bi", "ci", "di", False)

        for ch in range(2):
            o = 128 * ch
            v(lambda e, o=o, ch=ch:
              e.tensor_scalar(A["RP1"][:, o:o + 128], A["PT11"][:, o:o + 128],
                              mc("f1c", ch), None, OP.mult),
              reads=[B["PT11"], B["mcols"]], writes=[B["RP1"]])
            v(lambda e, o=o, ch=ch:
              e.scalar_tensor_tensor(A["RP1"][:, o:o + 128], A["PT12"][:, o:o + 128],
                                     mc("f2c", ch), A["RP1"][:, o:o + 128],
                                     OP.mult, OP.add),
              reads=[B["PT12"], B["mcols"]], writes=[B["RP1"]])
            v(lambda e, o=o, ch=ch:
              e.tensor_scalar(A["RP2"][:, o:o + 128], A["PT21"][:, o:o + 128],
                              mc("f1c", ch), None, OP.mult),
              reads=[B["PT21"], B["mcols"]], writes=[B["RP2"]])
            v(lambda e, o=o, ch=ch:
              e.scalar_tensor_tensor(A["RP2"][:, o:o + 128], A["PT22"][:, o:o + 128],
                                     mc("f2c", ch), A["RP2"][:, o:o + 128],
                                     OP.mult, OP.add),
              reads=[B["PT22"], B["mcols"]], writes=[B["RP2"]])
        yield
        parQ = yield from build_powers({"11": "PT11", "12": "PT12", "21": "PT21",
                             "22": "PT22"}, "ladQ", "a", "b", "c", "d", True)
        for src, dst in [("RP1", "R1"), ("RP2", "R2"),
                         ("PT21", "Q21s"), ("PT22", "Q22s")]:
            yield
            for ch in range(2):
                bank = B["P1"]
                self.mm(bank, PS[1][:, 0:128], A[src][:, 128 * ch:128 * (ch + 1)],
                        A["idf"][:], True, True, reads=[B[src], B["idf"]])
                for t in range(2):
                    v(lambda e, dst=dst, t=t, ch=ch:
                      e.tensor_copy(A[dst][:, 256 * t + 128 * ch:256 * t + 128 * (ch + 1)],
                                    PS[1][:, 0:128]),
                      reads=[bank], writes=[B[dst]])

        for i in range(4):
            yield
            for ch in range(2):
                bank = B["P2"]
                self.mm(bank, PS[2][0:1, 0:128],
                        A["ladQ"][:, 8 * parQ + 2 * i + ch:8 * parQ + 2 * i + ch + 1],
                        A["idf"][:], True, True, reads=[B["ladQ"], B["idf"]])
                for t in range(2):
                    v(lambda e, i=i, ch=ch, t=t:
                      e.tensor_copy(A["erow"][0:1, 512 * i + 256 * t + 128 * ch:
                                              512 * i + 256 * t + 128 * (ch + 1)],
                                    PS[2][0:1, 0:128]),
                      reads=[bank], writes=[B["erow"]])

    def xw(self, t):
        return (self.A["xw0"], self.B["xwA"]) if t % 2 == 0 else \
               (self.A["xw1"], self.B["xwB"])

    def load_xtile(self, t, src="x0"):
        xwA, xwB = self.xw(t)
        for j in range(NJ):
            self.dma("q", xwA[:, 1024 * j:1024 * (j + 1)],
                     self.D[src][TILE * t + 128 * j:TILE * t + 128 * (j + 1), :],
                     writes=[xwB[j]])
            self.q_fence()

    def u1slot(self, t, j):
        k = (4 * t + j) % self.u1mod
        if k < 4:
            return self.A["u1TM"][:, 1024 * k:1024 * (k + 1)], self.B["u1TM"][k]
        if k == 4:
            return self.A["h1p"][:], self.B["h1p"]
        return (self.A["u2t"][:, 1024 * (k - 5):1024 * (k - 4)],
                self.B["u2t"][k - 5])

    def ln_stats_j(self, t, j):
        A, B = self.A, self.B
        xwA, xwB = self.xw(t)
        self.s(lambda e, j=j: e.activation(A["u2FMt"][:], xwA[:, 1024 * j:1024 * (j + 1)],
                                           AF.Square, accum_out=A["statQ"][:, j:j + 1]),
               reads=[xwB[j]], writes=[B["u2FMt"], B["statQ"][j]])
        self.v(lambda e, j=j: e.reduce_sum(A["statS"][:, j:j + 1],
                                           xwA[:, 1024 * j:1024 * (j + 1)], axis=AX.X),
               reads=[xwB[j]], writes=[B["statS"][j]])

    def ln_statmath_a(self, t, j):
        A, B = self.A, self.B
        s1 = A["stt1"][:, j:j + 1]
        s2 = A["stt2"][:, j:j + 1]
        s3 = A["stt3"][:, j:j + 1]
        v = self.v
        v(lambda e: e.tensor_scalar(s1, A["statS"][:, j:j + 1], 1.0 / H, None, OP.mult),
          reads=[B["statS"][j]], writes=[B["stt1"][j]])
        v(lambda e: e.tensor_scalar(s2, A["statQ"][:, j:j + 1], 1.0 / H, EPS,
                                    OP.mult, OP.add),
          reads=[B["statQ"][j]], writes=[B["stt2"][j]])
        v(lambda e: e.tensor_tensor(s3, s1, s1, OP.mult),
          reads=[B["stt1"][j]], writes=[B["stt3"][j]])
        v(lambda e: e.tensor_sub(s2, s2, s3),
          reads=[B["stt3"][j]], writes=[B["stt2"][j]])
        v(lambda e: e.reciprocal(s3, s2),
          reads=[B["stt2"][j]], writes=[B["stt3"][j]])

    def ln_statmath_b(self, t, j, ac, bbc):
        A, B = self.A, self.B
        col = (4 * t + j) % 32
        acs = A[ac][:, col:col + 1]
        bbs = A[bbc][:, col:col + 1]
        self.s(lambda e: e.activation(acs, A["stt3"][:, j:j + 1], AF.Sqrt),
               reads=[B["stt3"][j]], writes=[B[ac]])
        self.v(lambda e: e.scalar_tensor_tensor(bbs, A["stt1"][:, j:j + 1],
                                                -1.0, acs, OP.mult, OP.mult),
               reads=[B["stt1"][j], B[ac]], writes=[B[bbc]])

    def ln_statmath_j(self, t, j, ac, bbc):
        self.ln_statmath_a(t, j)
        self.ln_statmath_b(t, j, ac, bbc)

    def u_build(self, t, j, uT, uTB, ac, bbc, dst_j=None, eng="g"):
        A, B = self.A, self.B
        xwA, xwB = self.xw(t)
        col = (4 * t + j) % 32
        if uT is None:
            dap, dbf = self.u1slot(t, j)
        else:
            dj = j if dst_j is None else dst_j
            dap, dbf = uT[:, 1024 * dj:1024 * (dj + 1)], uTB[dj]
        f = {"g": self.g, "v": self.v}[eng]
        f(lambda e: e.tensor_scalar(dap, xwA[:, 1024 * j:1024 * (j + 1)],
                                    A[ac][:, col:col + 1],
                                    A[bbc][:, col:col + 1],
                                    OP.mult, OP.add),
          reads=[xwB[j], B[ac], B[bbc]], writes=[dbf])

    def transpose_uFM_j(self, t, j, uT, uTB, negate=False):
        self.phase = f"tpuFM.t{t}.j{j}"
        """u1 (TM) -> uFM chunk-columns for this j (FFN banks P5/P6)."""
        A, B, PS = self.A, self.B, self.PS
        if uT is None:
            sap, sbf = self.u1slot(t, j)
        else:
            sap, sbf = uT[:, 1024 * j:1024 * (j + 1)], uTB[j]
        b0 = getattr(self, "tpu_banks", (5, 6))
        for half in range(2):
            pb = b0[half]
            bank = B[f"P{pb}"]
            for cc in range(4):
                c = 4 * half + cc
                self.mm(bank, PS[pb][:, 128 * cc:128 * (cc + 1)],
                        sap[:, 128 * c:128 * (c + 1)],
                        A["idb"][:], cc == 0, cc == 3, reads=[sbf, B["idb"]])
            for cc in range(4):
                c = 4 * half + cc
                fn = (lambda e, c=c, j=j, pb=pb, cc=cc:
                      e.tensor_copy(A["uFM"][:, 512 * c + 128 * j:512 * c + 128 * (j + 1)],
                                    PS[pb][:, 128 * cc:128 * (cc + 1)]))
                fns = (lambda e, c=c, j=j, pb=pb, cc=cc:
                       e.copy(A["uFM"][:, 512 * c + 128 * j:512 * c + 128 * (j + 1)],
                              PS[pb][:, 128 * cc:128 * (cc + 1)]))
                if cc % 2 == 0:
                    self.v(fn, reads=[bank], writes=[B["uFM"][c]])
                else:
                    self.s(fns, reads=[bank], writes=[B["uFM"][c]])
        if negate:
            self.v(lambda e, j=j: e.tensor_scalar(
                A["uFMn"][:, 128 * j:128 * (j + 1)],
                A["uFM"][:, 3072 + 128 * j:3072 + 128 * (j + 1)], -1.0, None, OP.mult),
                reads=[B["uFM"][6]], writes=[B["uFMn"]])
            self.v(lambda e, j=j: e.tensor_scalar(
                A["uFMn"][:, 512 + 128 * j:512 + 128 * (j + 1)],
                A["uFM"][:, 3584 + 128 * j:3584 + 128 * (j + 1)], -1.0, None, OP.mult),
                reads=[B["uFM"][7]], writes=[B["uFMn"]])
            self.v(lambda e, j=j: e.tensor_scalar(
                A["uFMn"][:, 1024 + 128 * j:1024 + 128 * (j + 1)],
                A["uFM"][:, 128 * j:128 * (j + 1)], -1.0, None, OP.mult),
                reads=[B["uFM"][0]], writes=[B["uFMn"]])
            self.v(lambda e, j=j: e.tensor_scalar(
                A["uFMn"][:, 1536 + 128 * j:1536 + 128 * (j + 1)],
                A["uFM"][:, 512 + 128 * j:512 + 128 * (j + 1)], -1.0, None, OP.mult),
                reads=[B["uFM"][1]], writes=[B["uFMn"]])

    def u2_prep(self, t, j):
        self.phase = f"u2prep.t{t}.j{j}"
        """u2 for scan chunk j of tile t: build u2t (TM) + transpose to
        u2FMt (feature-major) via SSM bank P0."""
        A, B, PS = self.A, self.B, self.PS
        aj = j % 3
        self.u_build(t, j, A["u2t"], B["u2t"], "a2c", "bb2c", dst_j=aj)
        u2B = B["u2t"][aj]
        for grp in range(2):
            bank = B["P0"]
            for cc in range(4):
                c = 4 * grp + cc
                self.mm(bank, PS[0][:, 128 * cc:128 * (cc + 1)],
                        A["u2t"][:, 1024 * aj + 128 * c:1024 * aj + 128 * (c + 1)],
                        A["idb"][:], cc == 0, cc == 3, reads=[u2B, B["idb"]])
            for cc in range(4):
                c = 4 * grp + cc
                if cc % 2 == 0:
                    self.v(lambda e, c=c, cc=cc:
                           e.tensor_copy(A["u2FMt"][:, 128 * c:128 * (c + 1)],
                                         PS[0][:, 128 * cc:128 * (cc + 1)]),
                           reads=[bank], writes=[B["u2FMt"]])
                else:
                    self.s(lambda e, c=c, cc=cc:
                           e.copy(A["u2FMt"][:, 128 * c:128 * (c + 1)],
                                  PS[0][:, 128 * cc:128 * (cc + 1)]),
                           reads=[bank], writes=[B["u2FMt"]])

    def ffn_main(self, t, uT, uTB, feeder=None):
        fl = f"ffn.{getattr(self, 'sweep_tag', '?')}.t{t}"
        def F():
            if feeder:
                ph = self.phase
                feeder()
                self.phase = ph
        """mA/mB + three coupling cycles -> abc coeff columns.
        Banks: P5/P6 for matmul groups, P7 for rows/bcast/coeffs.
        slot(i) is called at 4 TE-idle points to interleave other work."""
        A, B, PS = self.A, self.B, self.PS
        v, s, g, mm = self.v, self.s, self.g, self.mm
        cr = A["crow"]
        ab = A["abrow"]

        def mm1_chain(which):
            self.phase = f"{fl}.mm1{which}"
            for m in range(8):
                pi = 5 + m % 2
                bank = B[f"P{pi}"]
                for k in range(8):
                    lhsT = A["w1s"][:, 1024 * k + 128 * m:1024 * k + 128 * (m + 1)]
                    if which == "A":
                        mov, mb = A["uFM"][:, 512 * k:512 * (k + 1)], B["uFM"][k]
                    elif PSGN[k] > 0:
                        mov, mb = A["uFM"][:, 512 * PERM[k]:512 * (PERM[k] + 1)], \
                            B["uFM"][PERM[k]]
                    elif k in (2, 3):
                        mov, mb = A["uFMn"][:, 512 * (k - 2):512 * (k - 1)], B["uFMn"]
                    else:
                        mov, mb = A["uFMn"][:, 1024 + 512 * (k - 4):1024 + 512 * (k - 3)], \
                            B["uFMn"]
                    mm(bank, PS[pi][:, 0:512], lhsT, mov, k == 0, k == 7,
                       reads=[B["w1s"], mb])
                dst = "mA" if which == "A" else "mB"
                s(lambda e, m=m, dst=dst, pi=pi:
                  e.copy(A[dst][:, 512 * m:512 * (m + 1)], PS[pi][:, 0:512]),
                  reads=[bank], writes=[B[dst][m]])
                if m % 2 == 1:
                    F()
                if which == "A":
                    s(lambda e, m=m, pi=pi:
                      e.activation(A["h1"][:, 512 * m:512 * (m + 1)],
                                   PS[pi][:, 0:512], AF.Gelu,
                                   bias=A["b1c"][:, m:m + 1]),
                      reads=[bank, B["b1c"]], writes=[B["h1"][m]])

        def mm234(it):
            self.phase = f"{fl}.mm234.{it}"
            for m in range(4):
                bank = B[f"P{5 + m % 2}"]
                for k in range(8):
                    mm(bank, PS[5 + m % 2][:, 0:512],
                       A["w2s"][:, 512 * k + 128 * m:512 * k + 128 * (m + 1)],
                       A["h1"][:, 512 * k:512 * (k + 1)], k == 0, k == 7,
                       reads=[B["w2s"], B["h1"][k]])
                s(lambda e, m=m: e.activation(A["h2"][:, 512 * m:512 * (m + 1)],
                                              PS[5 + m % 2][:, 0:512], AF.Gelu,
                                              bias=A["b2c"][:, m:m + 1]),
                  reads=[bank, B["b2c"]], writes=[B["h2"][m]])
                F()
            for m in range(2):
                bank = B[f"P{5 + m % 2}"]
                for k in range(4):
                    mm(bank, PS[5 + m % 2][:, 0:512],
                       A["w3s"][:, 256 * k + 128 * m:256 * k + 128 * (m + 1)],
                       A["h2"][:, 512 * k:512 * (k + 1)], k == 0, k == 3,
                       reads=[B["w3s"], B["h2"][k]])
                s(lambda e, m=m: e.activation(A["h3"][:, 512 * m:512 * (m + 1)],
                                              PS[5 + m % 2][:, 0:512], AF.Gelu,
                                              bias=A["b3c"][:, m:m + 1]),
                  reads=[bank, B["b3c"]], writes=[B["h3"][m]])
                F()
            bank = B["P7"]
            for k in range(2):
                mm(bank, PS[7][0:1, 0:512], A["w4s"][:, k:k + 1],
                   A["h3"][:, 512 * k:512 * (k + 1)], k == 0, k == 1,
                   reads=[B["w4s"], B["h3"][k]])
            s(lambda e: e.activation(cr[0:1, 1536:2048], PS[7][0:1, 0:512],
                                     AF.Tanh, scale=0.5, bias=A["hb"][0:1, 0:1]),
              reads=[bank, B["hb"]], writes=[B["crow"]])
            v(lambda e, it=it:
              e.tensor_scalar(cr[0:1, 512 * it:512 * (it + 1)], cr[0:1, 1536:2048],
                              A["kap"][0:1, 1:2], A["kap"][0:1, 0:1],
                              OP.mult, OP.add),
              reads=[B["kap"]], writes=[B["crow"]])
            F()

        def combines(it):
            bank = B["P7"]
            if it == 1:
                v(lambda e: e.tensor_copy(A["brow"][0:1, 0:512], cr[0:1, 0:512]),
                  reads=[B["crow"]], writes=[B["brow"]])
                mm(bank, PS[7][:, 0:512], A["on1b"][:], A["brow"][0:1, 0:512],
                   True, True, reads=[B["on1b"], B["brow"]])
                v(lambda e: e.tensor_copy(A["cbcB"][:], PS[7][:, 0:512]),
                  reads=[bank], writes=[B["cbcB"]])
            else:
                v(lambda e: e.tensor_tensor(ab[0:1, 0:512], cr[0:1, 512:1024],
                                            cr[0:1, 0:512], OP.mult),
                  reads=[B["crow"]], writes=[B["abrow"]])
                v(lambda e: e.tensor_scalar(ab[0:1, 0:512], ab[0:1, 0:512],
                                            -1.0, 1.0, OP.mult, OP.add),
                  reads=[], writes=[B["abrow"]])
                v(lambda e: e.tensor_add(ab[0:1, 512:1024], cr[0:1, 0:512],
                                         cr[0:1, 512:1024]),
                  reads=[B["crow"]], writes=[B["abrow"]])
                v(lambda e: e.tensor_copy(A["brow"][0:1, 512:1024], ab[0:1, 0:512]),
                  reads=[B["abrow"]], writes=[B["brow"]])
                v(lambda e: e.tensor_copy(A["brow"][0:1, 1024:1536], ab[0:1, 512:1024]),
                  reads=[B["abrow"]], writes=[B["brow"]])
                mm(bank, PS[7][:, 0:512], A["on1b"][:], A["brow"][0:1, 512:1024],
                   True, True, reads=[B["on1b"], B["brow"]])
                v(lambda e: e.tensor_copy(A["cbcA"][:], PS[7][:, 0:512]),
                  reads=[bank], writes=[B["cbcA"]])
                mm(bank, PS[7][:, 0:512], A["on1b"][:], A["brow"][0:1, 1024:1536],
                   True, True, reads=[B["on1b"], B["brow"]])
                v(lambda e: e.tensor_copy(A["cbcB"][:], PS[7][:, 0:512]),
                  reads=[bank], writes=[B["cbcB"]])
            cbc2B = A["cbcB"][:].rearrange("p f -> p () f").broadcast_to([128, 2, 512])
            cbc2A = A["cbcA"][:].rearrange("p f -> p () f").broadcast_to([128, 2, 512])
            for pr in range(4):
                m0 = 2 * pr
                sl = lambda T, pr=pr: A[T][:, 1024 * pr:1024 * (pr + 1)].rearrange(
                    "p (r f) -> p r f", r=2)
                hv = A["h1"][:, 1024 * pr:1024 * (pr + 1)].rearrange(
                    "p (r f) -> p r f", r=2)
                sc = A["h2"][:, 1024 * (pr % 2):1024 * (pr % 2 + 1)].rearrange(
                    "p (r f) -> p r f", r=2)
                scb = [B["h2"][2 * (pr % 2)], B["h2"][2 * (pr % 2) + 1]]
                hb = [B["h1"][m0], B["h1"][m0 + 1]]
                eng = v if pr != 3 else g
                if it == 1:
                    eng(lambda e, sl=sl, hv=hv, cbc2B=cbc2B:
                        e.tensor_tensor(hv, sl("mB"), cbc2B, OP.mult),
                        reads=[B["mB"][m0], B["mB"][m0 + 1], B["cbcB"]],
                        writes=hb)
                    eng(lambda e, sl=sl, hv=hv:
                        e.tensor_tensor(hv, hv, sl("mA"), OP.add),
                        reads=[B["mA"][m0], B["mA"][m0 + 1]], writes=hb)
                else:
                    eng(lambda e, sl=sl, hv=hv, cbc2A=cbc2A:
                        e.tensor_tensor(hv, sl("mA"), cbc2A, OP.mult),
                        reads=[B["mA"][m0], B["mA"][m0 + 1], B["cbcA"]],
                        writes=hb)
                    eng(lambda e, sl=sl, sc=sc, cbc2B=cbc2B:
                        e.tensor_tensor(sc, sl("mB"), cbc2B, OP.mult),
                        reads=[B["mB"][m0], B["mB"][m0 + 1], B["cbcB"]],
                        writes=scb)
                    eng(lambda e, hv=hv, sc=sc:
                        e.tensor_tensor(hv, hv, sc, OP.add),
                        reads=scb, writes=hb)
                for mm_ in (m0, m0 + 1):
                    s(lambda e, m=mm_: e.activation(
                        A["h1"][:, 512 * m:512 * (m + 1)],
                        A["h1"][:, 512 * m:512 * (m + 1)], AF.Gelu,
                        bias=A["b1c"][:, m:m + 1]),
                      reads=[B["h1"][mm_], B["b1c"]], writes=[B["h1"][mm_]])

        mm1_chain("A")
        mm234(0)
        mm1_chain("B")
        combines(1)
        F()
        mm234(1)
        combines(2)
        F()
        mm234(2)

        v(lambda e: e.tensor_tensor(A["srow"][0:1, 0:512], cr[0:1, 1024:1536],
                                    ab[0:1, 512:1024], OP.mult),
          reads=[B["crow"], B["abrow"]], writes=[B["srow"]])
        v(lambda e: e.tensor_tensor(A["srow"][0:1, 512:1024], cr[0:1, 1024:1536],
                                    ab[0:1, 0:512], OP.mult),
          reads=[B["crow"], B["abrow"]], writes=[B["srow"]])
        v(lambda e: e.tensor_sub(cr[0:1, 0:512], ab[0:1, 0:512], A["srow"][0:1, 0:512]),
          reads=[B["abrow"], B["srow"]], writes=[B["crow"]])
        v(lambda e: e.tensor_add(cr[0:1, 512:1024], ab[0:1, 512:1024],
                                 A["srow"][0:1, 512:1024]),
          reads=[B["abrow"], B["srow"]], writes=[B["crow"]])
        bank = B["P7"]
        for jj in range(NJ):
            mm(bank, PS[7][:, jj:jj + 1], cr[0:1, 128 * jj:128 * (jj + 1)],
               A["cst"][0:1, 0:1], True, True, reads=[B["crow"], B["cst"]])
            mm(bank, PS[7][:, 4 + jj:5 + jj], cr[0:1, 512 + 128 * jj:512 + 128 * (jj + 1)],
               A["cst"][0:1, 0:1], True, True, reads=[B["crow"], B["cst"]])
            mm(bank, PS[7][:, 8 + jj:9 + jj], cr[0:1, 512 + 128 * jj:512 + 128 * (jj + 1)],
               A["cst"][0:1, 1:2], True, True, reads=[B["crow"], B["cst"]])
        v(lambda e: e.tensor_copy(A["abc"][:], PS[7][:, 0:12]),
          reads=[bank], writes=[B["abc"]])

    def ffn_apply_j(self, t, j, uT, uTB):
        A, B = self.A, self.B
        xwA, xwB = self.xw(t)
        v = self.v
        if uT is None:
            sap, sbf = self.u1slot(t, j)
        else:
            sap, sbf = uT[:, 1024 * j:1024 * (j + 1)], uTB[j]
        v(lambda e, j=j:
          e.tensor_scalar(A["stile"][:], sap,
                          A["abc"][:, j:j + 1], None, OP.mult),
          reads=[sbf, B["abc"]], writes=[*B["stile"]])
        for qq in range(4):
            colb = (4 + j) if SIGSGN[qq] > 0 else (8 + j)
            v(lambda e, j=j, qq=qq, colb=colb:
              e.scalar_tensor_tensor(A["stile"][:, 256 * qq:256 * (qq + 1)],
                                     sap[:, 256 * SIGMA[qq]:256 * (SIGMA[qq] + 1)],
                                     A["abc"][:, colb:colb + 1],
                                     A["stile"][:, 256 * qq:256 * (qq + 1)],
                                     OP.mult, OP.add),
              reads=[sbf, B["abc"]], writes=[B["stile"][qq // 2]])
        v(lambda e, j=j:
          e.tensor_add(xwA[:, 1024 * j:1024 * (j + 1)],
                       xwA[:, 1024 * j:1024 * (j + 1)], A["stile"][:]),
          reads=[*B["stile"]], writes=[xwB[j]])

    def gf_loads(self):
        A, B, D = self.A, self.B, self.D
        for k in range(8):
            self.dma("g", A["mA"][:, 512 * k:512 * (k + 1)],
                     D["gc_w1"][128 * k:128 * (k + 1), :], writes=[B["mA"][k]])
        for k in range(4):
            self.dma("g", A["mB"][:, 256 * k:256 * (k + 1)],
                     D["gc_w2"][128 * k:128 * (k + 1), :],
                     writes=[B["mB"][k // 2]])
        self.g_fence()

    def gf_chain(self, bbc="bb1c", loads=True):
        self.phase = "gf"
        A, B, PS, D = self.A, self.B, self.PS, self.D
        v, s, mm = self.v, self.s, self.mm
        if loads:
            self.gf_loads()
        bank = B["P5"]
        if bbc is not None:
            nbc = min(4 * self.NT, 32)
            v(lambda e: e.reduce_sum(A["stt4"][:, 4:5], A[bbc][:, 0:nbc], axis=AX.X),
              reads=[B[bbc]], writes=[B["stt4"][0]])
            mm(bank, PS[5][0:1, 8:9], A["stt4"][:, 4:5], A["oncf"][:], True, True,
               reads=[B["stt4"][0], B["oncf"]])
            v(lambda e: e.tensor_copy(A["bbsum"][:], PS[5][0:1, 8:9]),
              reads=[bank], writes=[B["bbsum"]])
            v(lambda e: e.tensor_scalar(A["sumrow"][:], A["sumrow"][:],
                                        A["bbsum"][0:1, 0:1], None, OP.add),
              reads=[B["bbsum"]], writes=[B["sumrow"]])
        for kk in range(8):
            mm(bank, PS[5][:, kk:kk + 1],
               A["sumrow"][0:1, 128 * kk:128 * (kk + 1)],
               A["cst"][0:1, 3:4], True, True, reads=[B["sumrow"], B["cst"]])
        v(lambda e: e.tensor_copy(A["meanu"][:], PS[5][:, 0:8]),
          reads=[bank], writes=[B["meanu"]])
        bank = B["P6"]
        for m in range(4):
            for k in range(8):
                mm(bank, PS[6][:, m:m + 1],
                   A["mA"][:, 512 * k + 128 * m:512 * k + 128 * (m + 1)],
                   A["meanu"][:, k:k + 1], k == 0, k == 7,
                   reads=[B["mA"][k], B["meanu"]])
        for m in range(4):
            s(lambda e, m=m: e.activation(A["gcol1"][:, m:m + 1], PS[6][:, m:m + 1],
                                          AF.Gelu, bias=A["gb1c"][:, m:m + 1]),
              reads=[bank, B["gb1c"]], writes=[B["gcol1"]])
        bank = B["P7"]
        for m in range(2):
            for k in range(4):
                mm(bank, PS[7][:, m:m + 1],
                   A["mB"][:, 256 * k + 128 * m:256 * k + 128 * (m + 1)],
                   A["gcol1"][:, k:k + 1], k == 0, k == 3,
                   reads=[B["mB"][k // 2], B["gcol1"]])
        for m in range(2):
            s(lambda e, m=m: e.activation(A["gcol2"][:, m:m + 1], PS[7][:, m:m + 1],
                                          AF.Gelu, bias=A["gb2c"][:, m:m + 1]),
              reads=[bank, B["gb2c"]], writes=[B["gcol2"]])
        bank = B["P5"]
        for k in range(2):
            mm(bank, PS[5][0:1, 0:1], A["gc3"][:, k:k + 1], A["gcol2"][:, k:k + 1],
               k == 0, k == 1, reads=[B["gc3"], B["gcol2"]])
        s(lambda e: e.activation(A["gfv"][:], PS[5][0:1, 0:1], AF.Tanh,
                                 scale=0.5, bias=A["hb"][0:1, 1:2]),
          reads=[bank, B["hb"]], writes=[B["gfv"]])
        v(lambda e: e.tensor_scalar(A["kap"][0:1, 1:2], A["arw"][:], 0.3, None, OP.mult),
          reads=[B["arw"]], writes=[B["kap"]])
        v(lambda e: e.tensor_tensor(A["kap"][0:1, 0:1], A["arw"][:], A["gfv"][:], OP.mult),
          reads=[B["arw"], B["gfv"]], writes=[B["kap"]])
        v(lambda e: e.tensor_scalar(A["kap"][0:1, 0:1], A["kap"][0:1, 0:1],
                                    0.7, 0.1, OP.mult, OP.add),
          reads=[], writes=[B["kap"]])
        v(lambda e: e.memset(A["sumrow"][:], 0.0), writes=[B["sumrow"]])

    def sum_u(self, slot, pi=3):
        self.phase = f"sumu.j{slot}"
        A, B, PS = self.A, self.B, self.PS
        bank = B[f"P{pi}"]
        for hh in range(2):
            self.mm(bank, PS[pi][0:1, 0:512], A["oncb"][:],
                    A["u2t"][:, 1024 * slot + 512 * hh:1024 * slot + 512 * (hh + 1)],
                    True, True, reads=[B["oncb"], B["u2t"][slot]])
            self.v(lambda e, hh=hh:
                   e.tensor_add(A["sumrow"][0:1, 512 * hh:512 * (hh + 1)],
                                A["sumrow"][0:1, 512 * hh:512 * (hh + 1)],
                                PS[pi][0:1, 0:512]),
                   reads=[bank], writes=[B["sumrow"]])

    def sum_x(self, t, j, ac, pi=3):
        self.phase = f"sumu.j{j}"
        A, B, PS = self.A, self.B, self.PS
        xwA, xwB = self.xw(t)
        col = (4 * t + j) % 32
        bank = B[f"P{pi}"]
        for hh in range(2):
            self.mm(bank, PS[pi][0:1, 0:512], A[ac][:, col:col + 1],
                    xwA[:, 1024 * j + 512 * hh:1024 * j + 512 * (hh + 1)],
                    True, True, reads=[B[ac], xwB[j]])
            self.v(lambda e, hh=hh:
                   e.tensor_add(A["sumrow"][0:1, 512 * hh:512 * (hh + 1)],
                                A["sumrow"][0:1, 512 * hh:512 * (hh + 1)],
                                PS[pi][0:1, 0:512]),
                   reads=[bank], writes=[B["sumrow"]])

    def sweep1(self):
        self.sweep_tag = "w1"
        self.u1mod = 5
        """Per-chunk LN1 pipeline (stats | statmath | u-build+sum skewed by
        one chunk each) with setup2's serial ladder work fed into the gaps."""
        A, B = self.A, self.B
        uTB = B["u1TM"]
        gen = self.setup2_steps()

        def F():
            ph = self.phase
            next(gen, None)
            self.phase = ph
        self.load_xtile(0)
        if self.NT > 1:
            self.load_xtile(1)
        for _ in range(27):
            F()
        self.gf_loads()
        seq = [(t, j) for t in range(self.NT) for j in range(NJ)]
        n = len(seq)
        for idx in range(n + 3):
            if idx < n:
                t, j = seq[idx]
                self.ln_stats_j(t, j)
            F()
            if 1 <= idx and idx - 1 < n:
                t, j = seq[idx - 1]
                self.ln_statmath_a(t, j)
            F()
            if 2 <= idx and idx - 2 < n:
                t, j = seq[idx - 2]
                self.ln_statmath_b(t, j, "a1c", "bb1c")
            if 3 <= idx and idx - 3 < n:
                t, j = seq[idx - 3]
                self.sum_x(t, j, "a1c")
                if j == NJ - 1 and t + 2 < self.NT:
                    self.load_xtile(t + 2)
        for _ in gen:
            pass

    def sweep2(self):
        self.sweep_tag = "w2"
        self.u1mod = 5
        """Software-pipelined: FFN of tile t runs while the SSM of tile t-1
        is interleaved into the FFN's TE slots.  FFN: banks P5-P7; SSM:
        banks P0-P4.  Tail (LN3/stores) of t-1 is emitted right after
        ffn_main(t) so the next tile's loads/u-builds/transposes fill the
        coefficient-latency at the FFN boundary."""
        A, B, D = self.A, self.B, self.D
        self.load_xtile(0)
        if self.NT > 1:
            self.load_xtile(1)
        for j in range(NJ):
            self.u_build(0, j, None, None, "a1c", "bb1c")
            self.transpose_uFM_j(0, j, None, None, negate=True)
        self.gf_chain(loads=False)
        for t in range(self.NT + 1):
            if t < self.NT:
                xwA, xwB = self.xw(t)

                gen = self.ssm_steps(t - 1) if t >= 1 else None
                feeder = (lambda g=gen: next(g, None)) if gen else None
                self.ffn_main(t, None, None, feeder)
                if gen is not None:
                    for _ in gen:
                        pass
            else:
                for _ in self.ssm_steps(t - 1):
                    pass
            if t < self.NT:
                for j in range(NJ):
                    if t + 1 < self.NT:
                        self.u_build(t + 1, j, None, None, "a1c", "bb1c",
                                     eng="v")
                        self.transpose_uFM_j(t + 1, j, None, None, negate=True)
                    self.ffn_apply_j(t, j, None, None)
                    if self.debug:
                        self.dma("q", D["dbg_x1"][TILE * t + 128 * j:TILE * t + 128 * (j + 1), :],
                                 xwA[:, 1024 * j:1024 * (j + 1)], reads=[xwB[j]])
                        self.q_fence()
                    self.ln_stats_j(t, j)
                for j in range(NJ):
                    self.ln_statmath_j(t, j, "a2c", "bb2c")

    def ssm_steps(self, t):
        A, B, PS, D = self.A, self.B, self.PS, self.D
        v, g, mm = self.v, self.g, self.mm
        xwA, xwB = self.xw(t)
        G1 = A["G1s"][:, 0:512]
        G2 = A["G2s"][:, 0:512]
        GB1, GB2 = B["G1s"], B["G2s"]
        h2sl = A["h2s"][:, 0:512]
        h2B = B["h2s"]
        rw, tr, sr, er = A["rw"], A["trow"], A["srow"], A["erow"]
        RW1, RW2 = B["rw"][0], B["rw"][1]
        for j in range(NJ):
            if j >= 2:
                self.sum_u((j - 2 + 2) % 3)
            self.u2_prep(t, j)
            yield
            self.phase = f"ssm.t{t}.j{j}.Bu"
            bank0 = B["P0"]
            for k in range(8):
                mm(bank0, PS[0][:, 0:512],
                   A["u2FMt"][:, 128 * k:128 * (k + 1)],
                   A["Bh"][:, 512 * k:512 * (k + 1)], k == 0, k == 7,
                   reads=[B["u2FMt"], B["Bh"]])
            v(lambda e: e.tensor_mul(G1, PS[0][:, 0:512], A["R1"][:]),
              reads=[bank0, B["R1"]], writes=[GB1])
            v(lambda e: e.tensor_mul(G2, PS[0][:, 0:512], A["R2"][:]),
              reads=[bank0, B["R2"]], writes=[GB2])
            yield
            self.phase = f"ssm.t{t}.j{j}.cs"
            b3, b4 = B["P3"], B["P4"]
            mm(b3, PS[3][0:1, 0:512], A["oncf"][:], G1, True, True,
               reads=[B["oncf"], GB1])
            mm(b4, PS[4][0:1, 0:512], A["oncf"][:], G2, True, True,
               reads=[B["oncf"], GB2])
            b1, b2 = B["P1"], B["P2"]
            mm(b1, PS[1][:, 0:512], A["ltri_s"][:], G1, True, False,
               reads=[B["ltri_s"], GB1])
            mm(b1, PS[1][:, 0:512], A["on1f"][:], rw[0:1, 0:512], False, True,
               reads=[B["on1f"], RW1])
            mm(b2, PS[2][:, 0:512], A["ltri_s"][:], G2, True, False,
               reads=[B["ltri_s"], GB2])
            mm(b2, PS[2][:, 0:512], A["on1f"][:], rw[0:1, 512:1024], False, True,
               reads=[B["on1f"], RW2])
            v(lambda e: e.tensor_add(tr[0:1, 0:512], PS[3][0:1, 0:512], rw[0:1, 0:512]),
              reads=[b3, RW1], writes=[B["trow"]])
            v(lambda e: e.tensor_add(tr[0:1, 512:1024], PS[4][0:1, 0:512], rw[0:1, 512:1024]),
              reads=[b4, RW2], writes=[B["trow"]])
            v(lambda e: e.tensor_tensor(rw[0:1, 0:512], tr[0:1, 0:512], er[0:1, 0:512], OP.mult),
              reads=[B["trow"]], writes=[RW1])
            v(lambda e: e.tensor_tensor(sr[0:1, 0:512], tr[0:1, 512:1024],
                                        er[0:1, 512:1024], OP.mult),
              reads=[B["trow"]], writes=[B["srow"]])
            v(lambda e: e.tensor_add(rw[0:1, 0:512], rw[0:1, 0:512], sr[0:1, 0:512]),
              reads=[B["srow"]], writes=[RW1])
            g(lambda e: e.tensor_tensor(rw[0:1, 512:1024], tr[0:1, 0:512],
                                        er[0:1, 1024:1536], OP.mult),
              reads=[B["trow"]], writes=[RW2])
            g(lambda e: e.tensor_tensor(sr[0:1, 512:1024], tr[0:1, 512:1024],
                                        er[0:1, 1536:2048], OP.mult),
              reads=[B["trow"]], writes=[B["srow2"]])
            g(lambda e: e.tensor_add(rw[0:1, 512:1024], rw[0:1, 512:1024],
                                     sr[0:1, 512:1024]),
              reads=[B["srow2"]], writes=[RW2])
            yield
            self.phase = f"ssm.t{t}.j{j}.h2"
            v(lambda e: e.tensor_mul(G1, PS[1][:, 0:512], A["Q21s"][:]),
              reads=[b1, B["Q21s"]], writes=[GB1])
            v(lambda e: e.tensor_mul(G2, PS[2][:, 0:512], A["Q22s"][:]),
              reads=[b2, B["Q22s"]], writes=[GB2])
            v(lambda e: e.tensor_add(h2sl, G1, G2), reads=[GB1, GB2], writes=[h2B])
            b5 = B["P4"]
            for qq in range(4):
                mm(b5, PS[4][:, 128 * qq:128 * (qq + 1)],
                   A["h2s"][:, 128 * qq:128 * (qq + 1)],
                   A["idb"][:], qq == 0, qq == 3, reads=[h2B, B["idb"]])
            for qq in range(4):
                if qq % 2 == 0:
                    self.v(lambda e, qq=qq, j=j:
                           e.tensor_copy(A["ysT"][:, 512 * qq + 128 * j:512 * qq + 128 * (j + 1)],
                                         PS[4][:, 128 * qq:128 * (qq + 1)]),
                           reads=[b5], writes=[B["ysT"]])
                else:
                    self.s(lambda e, qq=qq, j=j:
                           e.copy(A["ysT"][:, 512 * qq + 128 * j:512 * qq + 128 * (j + 1)],
                                  PS[4][:, 128 * qq:128 * (qq + 1)]),
                           reads=[b5], writes=[B["ysT"]])
            yield
            self.phase = f"ssm.t{t}.j{j}.Cy"
            for hh in range(2):
                bank = B["P3"]
                for qq in range(4):
                    mm(bank, PS[3][:, 0:512],
                       A["ysT"][:, 512 * qq + 128 * j:512 * qq + 128 * (j + 1)],
                       A["Ch"][:, 1024 * qq + 512 * hh:1024 * qq + 512 * (hh + 1)],
                       qq == 0, qq == 3, reads=[B["ysT"], B["Ch"]])
                g(lambda e, hh=hh, j=j:
                  e.tensor_mul(A["stile"][:, 512 * hh:512 * (hh + 1)],
                               A["u2t"][:, 1024 * (j % 3) + 512 * hh:
                                         1024 * (j % 3) + 512 * (hh + 1)],
                               A["Dbc"][:, 512 * hh:512 * (hh + 1)]),
                  reads=[B["u2t"][j % 3], B["Dbc"]], writes=[B["stile"][hh]])
                v(lambda e, hh=hh, j=j:
                  e.tensor_add(xwA[:, 1024 * j + 512 * hh:1024 * j + 512 * (hh + 1)],
                               xwA[:, 1024 * j + 512 * hh:1024 * j + 512 * (hh + 1)],
                               PS[3][:, 0:512]),
                  reads=[bank], writes=[xwB[j]])
                v(lambda e, hh=hh, j=j:
                  e.tensor_add(xwA[:, 1024 * j + 512 * hh:1024 * j + 512 * (hh + 1)],
                               xwA[:, 1024 * j + 512 * hh:1024 * j + 512 * (hh + 1)],
                               A["stile"][:, 512 * hh:512 * (hh + 1)]),
                  reads=[B["stile"][hh]], writes=[xwB[j]])
                yield
            self.ln_stats_j(t, j)
            self.ln_statmath_j(t, j, "a3c", "bb3c")
            self.dma("q", D["x2scr"][TILE * t + 128 * j:TILE * t + 128 * (j + 1), :],
                     xwA[:, 1024 * j:1024 * (j + 1)], reads=[xwB[j]])
            if self.debug:
                self.dma("q", D["dbg_x2"][TILE * t + 128 * j:TILE * t + 128 * (j + 1), :],
                         xwA[:, 1024 * j:1024 * (j + 1)], reads=[xwB[j]])
            self.u_build(t, j, A["u2t"], B["u2t"], "a3c", "bb3c",
                         dst_j=(j + 2) % 3)
            yield
        self.sum_u((NJ - 2 + 2) % 3)
        self.sum_u((NJ - 1 + 2) % 3)
        self.q_fence()
        if t + 2 < self.NT:
            self.load_xtile(t + 2)

    def w3_steps(self, tn):
        for _ in range(14):
            yield
        for j in range(NJ):
            self.u_build(tn, j, None, None, "a3c", "bb3c")
            yield
            self.transpose_uFM_j(tn, j, None, None, negate=True)
            yield

    def sweep3(self):
        self.sweep_tag = "w3"
        self.u1mod = 8
        self.tpu_banks = (0, 1)
        A, B, D = self.A, self.B, self.D
        self.load_xtile(0, src="x2scr")
        if self.NT > 1:
            self.load_xtile(1, src="x2scr")
        for j in range(NJ):
            self.u_build(0, j, None, None, "a3c", "bb3c")
            self.transpose_uFM_j(0, j, None, None, negate=True)
        self.gf_chain(None)
        for t in range(self.NT):
            xwA, xwB = self.xw(t)
            gen = self.w3_steps(t + 1) if t + 1 < self.NT else None
            feeder = (lambda g=gen: next(g, None)) if gen else None
            self.ffn_main(t, None, None, feeder)
            if gen is not None:
                for _ in gen:
                    pass
            for j in range(NJ):
                self.ffn_apply_j(t, j, None, None)
                self.dma("q", D["y"][TILE * t + 128 * j:TILE * t + 128 * (j + 1), :],
                         xwA[:, 1024 * j:1024 * (j + 1)], reads=[xwB[j]])
            self.q_fence()
            if t + 2 < self.NT:
                self.load_xtile(t + 2, src="x2scr")


def _make_consts(L):
    bf = ml_dtypes.bfloat16
    return {
        "ltri": np.triu(np.ones((128, 128), np.float32)),
        "identf": np.eye(128, dtype=np.float32),
        "identb": np.eye(128).astype(bf),
        "onescol_f": np.ones((128, 1), np.float32),
        "onescol_b": np.ones((128, 1)).astype(bf),
        "onesrow_b": np.ones((1, 128)).astype(bf),
        "onesrow_f": np.ones((1, 128), np.float32),
        "consts": np.array([[0.5, -0.5, 1.0, 1.0 / L]], np.float32),
    }


def make_inmap(inputs, core, L=4096):
    asc = lambda a: np.ascontiguousarray(np.asarray(a, dtype=np.float32))
    x = np.asarray(inputs["L0_attn_input_sequence"])
    m = {
        "x0": asc(x[core][:L]),
        "cn_w1": asc(inputs["L0_ffn_cn_w1"]), "cn_b1": asc(inputs["L0_ffn_cn_b1"]),
        "cn_w2": asc(inputs["L0_ffn_cn_w2"]), "cn_b2": asc(inputs["L0_ffn_cn_b2"]),
        "cn_w3": asc(inputs["L0_ffn_cn_w3"]), "cn_b3": asc(inputs["L0_ffn_cn_b3"]),
        "cn_w4": asc(inputs["L0_ffn_cn_w4"]), "cn_b4": asc(inputs["L0_ffn_cn_b4"]),
        "gc_w1": asc(inputs["L0_ffn_gc_w1"]), "gc_b1": asc(inputs["L0_ffn_gc_b1"]),
        "gc_w2": asc(inputs["L0_ffn_gc_w2"]), "gc_b2": asc(inputs["L0_ffn_gc_b2"]),
        "gc_w3": asc(inputs["L0_ffn_gc_w3"]), "gc_b3": asc(inputs["L0_ffn_gc_b3"]),
        "ar": asc(inputs["L0_ffn_adaptive_range"]),
        "steps_raw": asc(inputs["L0_attn_steps_raw"]),
        "G_raw": asc(inputs["L0_attn_G_raw"]),
        "A_raw": asc(inputs["L0_attn_A_raw"]),
        "B2": asc(np.asarray(inputs["L0_attn_B_mat"]).reshape(256, 2048)),
        "C2": asc(np.asarray(inputs["L0_attn_C_mat"]).reshape(1024, 512)),
        "D_vec": asc(inputs["L0_attn_D_vec"]),
    }
    m.update(_make_consts(L))
    return m


def kernel(**inputs):
    kb = KB(n_tiles=8, debug=False)
    nc = kb.build()
    in_maps = [make_inmap(inputs, c) for c in range(8)]
    res = run_bass_kernel_spmd(nc, in_maps, core_ids=list(range(8)))
    return np.stack([np.asarray(r["y"]) for r in res.results], axis=0)
